# version 14
# baseline (speedup 1.0000x reference)
import contextlib
import numpy as np
import concourse.bass as bass
import concourse.mybir as mybir
import bass_rust
from concourse.bass_utils import run_bass_kernel_spmd
from concourse.alu_op_type import AluOpType

F32 = mybir.dt.float32
BF16 = mybir.dt.bfloat16
FP16 = mybir.dt.float16
I32 = mybir.dt.int32
AF = mybir.ActivationFunctionType
OP = AluOpType

P = 128
S = 2048
D = 1024
HF = 4096
DCH = D // P
HCH = HF // P
NQ = 512
NQB = S // NQ
HALF = 1024
HEADS = 16
HD = 64
EPS = 1e-5

C1 = float(np.float32(1.0 / np.log(2.0)))
CHI = float(np.float32(np.log(2.0)))
CLO = float(np.float32(np.log(2.0) - np.float64(np.float32(np.log(2.0)))))

EBPAD = 640
EBW = EBPAD + S


def build_nc(dbg=()):
    nc = bass.Bass("TRN2", target_bir_lowering=False, detect_race_conditions=False)

    def di(name, shape):
        return nc.dram_tensor(name, shape, F32, kind="ExternalInput").ap()
    xT = di("xT", [D, S])
    w1T = di("w1T", [D, HF])
    w2T = di("w2T", [HF, D])
    Wq = di("Wq", [D, D])
    Wkv = di("Wkv", [D, 2 * HD])
    Wo = di("Wo", [D, D])
    pbw1 = di("pbw1", [D, D])
    pbw2 = di("pbw2", [D, HEADS])
    lng = di("lng", [P, DCH]); lnb = di("lnb", [P, DCH])
    olng = di("olng", [P, DCH]); olnb = di("olnb", [P, DCH])
    b1f = di("b1f", [P, HCH]); b2f = di("b2f", [P, DCH])
    pw0b0 = di("pw0b0", [2, D])
    pl0g = di("pl0g", [P, DCH]); pl0b = di("pl0b", [P, DCH])
    pl1g = di("pl1g", [P, DCH]); pl1b = di("pl1b", [P, DCH])
    pb1f = di("pb1f", [P, DCH])
    pb2r = di("pb2r", [HEADS, 1])
    posone = di("posone", [2, S])
    nabr = di("nabr", [1, HEADS])
    knullT = di("knullT", [HD, 1])
    vnullr = di("vnullr", [1, HD])
    c1024 = di("c1024", [P, P])
    onesr = di("onesr", [1, HD])

    o_out = nc.dram_tensor("o", [D, S], F32, kind="ExternalOutput").ap()
    ebtab = nc.dram_tensor("ebtab", [HEADS, EBW], BF16, kind="Internal").ap()
    wqTd = nc.dram_tensor("wqTd", [HF, D], BF16, kind="Internal").ap()
    w1Tb = nc.dram_tensor("w1Tb", [D, HF], BF16, kind="Internal").ap()
    Wqb = nc.dram_tensor("Wqb", [D, D], BF16, kind="Internal").ap()
    Wkvb = nc.dram_tensor("Wkvb", [D, 2 * HD], BF16, kind="Internal").ap()
    Wob = nc.dram_tensor("Wob", [D, D], BF16, kind="Internal").ap()
    pbw1b = nc.dram_tensor("pbw1b", [D, D], BF16, kind="Internal").ap()
    pbw2b = nc.dram_tensor("pbw2b", [D, HEADS], BF16, kind="Internal").ap()

    es = contextlib.ExitStack()
    plan = {k: [] for k in "TVSYG"}
    cnt = {}

    class Sem:
        def __init__(self, name):
            self.h = es.enter_context(nc.semaphore(name))
            cnt[name] = 0
            self.name = name
        @property
        def c(self):
            return cnt[self.name]
        def bump(self, k):
            cnt[self.name] += k
            return cnt[self.name]

    def wait(eng, sem, val):
        if val > 0:
            plan[eng].append(lambda e, s=sem, v=val: e.wait_ge(s.h, v))

    with es:
        def sbt(name, shape, dt):
            return es.enter_context(nc.sbuf_tensor(name, shape, dt))

        x = sbt("x", [P, DCH * S], BF16)
        xn = sbt("xn", [P, DCH * S], BF16)
        arena = sbt("arena", [P, 32768], BF16)
        kvT = sbt("kvT", [P, S], BF16)
        kkB = sbt("kkB", [P, S], BF16)
        vblk = sbt("vblk", [P, 17 * 65], BF16)
        knull2 = sbt("knull2", [P, P], BF16)
        xsq = sbt("xsq", [P, DCH * NQ], BF16)
        B0 = sbt("B0", [P, HALF], BF16)
        B3 = sbt("B3", [P, HALF], BF16)
        B1 = sbt("B1", [P, HALF], BF16)
        B2 = sbt("B2", [P, HALF], BF16)
        H0 = sbt("H0", [P, HALF], FP16)
        H1 = sbt("H1", [P, HALF], FP16)
        H2 = sbt("H2", [P, HALF], FP16)
        H3 = sbt("H3", [P, HALF], FP16)
        I0 = sbt("I0", [P, HALF], I32)
        F0 = sbt("F0", [P, HALF], F32)
        F1 = sbt("F1", [P, HALF], F32)
        F2 = sbt("F2", [P, HALF], F32)
        F3 = sbt("F3", [P, HALF], F32)
        mu_s = sbt("mu_s", [P, NQ], F32)
        m2_s = sbt("m2_s", [P, NQ], F32)
        t0_s = sbt("t0_s", [P, NQ], F32)
        t1_s = sbt("t1_s", [P, NQ], F32)
        rr_s = sbt("rr_s", [P, NQ], F32)
        pA = sbt("pA", [P, NQ], BF16)
        pB = sbt("pB", [P, NQ], BF16)
        peA = sbt("peA", [P, NQ], BF16)
        peB = sbt("peB", [P, NQ], BF16)
        ebtA = [sbt(f"ebtA{i}", [P, NQ], BF16) for i in range(2)]
        ebtB = [sbt(f"ebtB{i}", [P, NQ], BF16) for i in range(2)]
        tav = sbt("tav", [HD, NQ], BF16)
        tavn = sbt("tavn", [HD, NQ], BF16)
        c1024b = sbt("c1024b", [P, P], BF16)
        onesrF = sbt("onesrF", [1, HD], F32)
        lngS = sbt("lngS", [P, DCH], F32); lnbS = sbt("lnbS", [P, DCH], F32)
        olngS = sbt("olngS", [P, DCH], F32); olnbS = sbt("olnbS", [P, DCH], F32)
        b1S = sbt("b1S", [P, HCH], F32); b2S = sbt("b2S", [P, DCH], F32)
        bqS = sbt("bqS", [P, DCH], F32)
        pw0b0S = sbt("pw0b0S", [2, D], BF16)
        pl0gS = sbt("pl0gS", [P, DCH], F32); pl0bS = sbt("pl0bS", [P, DCH], F32)
        pl1gS = sbt("pl1gS", [P, DCH], F32); pl1bS = sbt("pl1bS", [P, DCH], F32)
        pb1S = sbt("pb1S", [P, DCH], F32)
        pb2S = sbt("pb2S", [HEADS, 1], F32)
        nabS = sbt("nabS", [1, HEADS], F32)
        nullm = sbt("nullm", [P, HEADS], F32)
        vnS = sbt("vnS", [1, HD], BF16)
        b0c = sbt("b0c", [P, 1], F32)
        epsc = sbt("epsc", [P, 1], F32)
        wb = [sbt(f"wb{i}", [P, NQ], BF16) for i in range(2)]

        psum = [es.enter_context(nc.psum_tensor(f"ps{i}", [P, NQ], F32))
                for i in range(8)]

        dS = Sem("dS")
        dG = Sem("dG")
        sT = Sem("sT")
        sV = Sem("sV")
        sS = Sem("sS")

        def dma(queue, out, in_, sem):
            tgt = sem.bump(16)
            plan[queue].append(
                lambda e, o=out, i=in_, s=sem: e.dma_start(out=o, in_=i).then_inc(s.h, 16))
            return tgt

        def dma_tr(queue, out, in_, sem):
            tgt = sem.bump(16)
            plan[queue].append(
                lambda e, o=out, i=in_, s=sem: e.dma_start_transpose(out=o, in_=i).then_inc(s.h, 16))
            return tgt

        def _mk(plankey, sem, meth, kw):
            tgt = sem.bump(1) if sem else None
            def fn(e, meth=meth, kw=kw, sem=sem):
                ins = getattr(e, meth)(**kw)
                if sem is not None:
                    ins.then_inc(sem.h, 1)
            plan[plankey].append(fn)
            return tgt

        def v_op(meth, sem=None, **kw):
            return _mk("V", sem, meth, kw)

        def s_op(meth, sem=None, **kw):
            return _mk("S", sem, meth, kw)

        def t_op(meth, sem=None, **kw):
            return _mk("T", sem, meth, kw)

        def toep(off):
            a = ebtab.copy()
            a.ap = bass_rust.VecI64Pair([(1, P), (1, NQ)])
            a.offset = off
            return a

        def dump_sb(name, src_tensor, cols, parts=P, dt=BF16):
            if name not in dbg:
                return
            t = nc.dram_tensor("dbg_" + name, [parts, cols], dt,
                               kind="ExternalOutput").ap()
            for e in "TVS":
                wait("Y", sT, sT.c); wait("Y", sV, sV.c); wait("Y", sS, sS.c)
            wait("Y", dG, dG.c)
            dma("Y", t, src_tensor[0:parts, 0:cols], dS)
            tgt = dS.c
            wait("V", dS, tgt); wait("S", dS, tgt); wait("T", dS, tgt)
            wait("G", dS, tgt)

        def dump_x(name):
            if name not in dbg:
                return
            t = nc.dram_tensor("dbg_" + name, [D, S], BF16,
                               kind="ExternalOutput").ap()
            wait("Y", sT, sT.c); wait("Y", sV, sV.c); wait("Y", sS, sS.c)
            for c in range(DCH):
                dma("Y", t[c * P:(c + 1) * P, :], x[:, c * S:(c + 1) * S], dS)
            tgt = dS.c
            wait("V", dS, tgt); wait("S", dS, tgt); wait("T", dS, tgt)
            wait("G", dS, tgt)

        for c in range(DCH):
            dma("G", x[:, c * S:(c + 1) * S], xT[c * P:(c + 1) * P, :], dG)
        dma("G", c1024b[:], c1024, dG)
        dma("G", pw0b0S[:], pw0b0, dG)
        dma("G", kvT[0:2, :], posone, dG)
        dma("G", vnS[:], vnullr, dG)
        dma("G", knull2[0:HD, 0:1], knullT, dG)
        dma("G", knull2[HD:P, 0:1], knullT, dG)
        ld_g = dG.c
        for dst, src in [(lngS, lng), (lnbS, lnb), (olngS, olng), (olnbS, olnb),
                         (b1S, b1f), (b2S, b2f), (pl0gS, pl0g), (pl0bS, pl0b),
                         (pl1gS, pl1g), (pl1bS, pl1b), (pb1S, pb1f), (pb2S, pb2r),
                         (nabS, nabr), (onesrF, onesr)]:
            dma("Y", dst[:], src, dS)
        ld_sync = dS.c

        def fn_preload(e):
            e.add_instruction(mybir.InstLoadActFuncSet(
                name=e.bass.get_next_instruction_name(),
                act_func_set_id=6, ins=[], outs=[]))
        plan["S"].append(fn_preload)

        wait("V", dG, ld_g)
        wait("V", dS, ld_sync)
        v_op("memset", ap=b0c[:], constant=0.0)
        v_op("memset", ap=epsc[:], constant=float(EPS))
        v_op("memset", ap=kvT[32:48, 0:EBPAD], constant=0.0)
        v_op("memset", ap=vblk[:], constant=1.0)
        v_op("memset", ap=vblk[:, 16 * 65:17 * 65], constant=0.0)
        v_op("memset", ap=vblk[0:1, 16 * 65 + HD:16 * 65 + HD + 1], constant=1.0)
        v_op("memset", ap=knull2[:, 1:P], constant=0.0)
        v_op("memset", ap=nullm[:], constant=0.0)
        init_v = v_op("tensor_copy", sem=sV,
                      out=vblk[0:1, 16 * 65:16 * 65 + HD], in_=vnS[:])
        wait("S", dS, ld_sync)
        wait("S", sV, init_v)
        s_init = s_op("activation", sem=sS, out=nabS[:], in_=nabS[:], func=AF.Exp,
                      scale=1.0, bias=b0c[0:1, :])
        wait("V", sS, s_init)
        init_v2 = v_op("tensor_copy", sem=sV, out=nullm[0:1, :], in_=nabS[:])
        dump_sb("nullmd", nullm, HEADS, dt=F32)
        dump_sb("nabSd", nabS, HEADS, parts=1, dt=F32)

        def chain_core(n, prescale_ln, sem_u):
            ltgt = s_op("activation", sem=sS, out=F0[:, :n], in_=B1[:, :n],
                        func=AF.Ln, scale=prescale_ln, bias=b0c[:])
            wait("V", sS, ltgt)
            v_op("tensor_scalar", out=F1[:, :n], in0=F0[:, :n], scalar1=-C1,
                 scalar2=1059.5, op0=OP.mult, op1=OP.add)
            v_op("tensor_copy", out=I0[:, :n], in_=F1[:, :n])
            v_op("tensor_copy", out=H1[:, :n], in_=I0[:, :n])
            v_op("tensor_tensor", out=H2[:, :n], in0=F1[:, :n], in1=H1[:, :n],
                 op=OP.subtract)
            v_op("scalar_tensor_tensor", out=H0[:, :n], in0=H2[:, :n],
                 scalar=0.5, in1=H1[:, :n], op0=OP.is_gt, op1=OP.add)
            v_op("scalar_tensor_tensor", out=H1[:, :n], in0=H2[:, :n],
                 scalar=-0.5, in1=H0[:, :n], op0=OP.is_ge, op1=OP.add)
            v_op("tensor_scalar", out=H2[:, :n], in0=H1[:, :n], scalar1=1.0,
                 scalar2=-1060.0, op0=OP.mult, op1=OP.add)
            v_op("tensor_scalar", out=F1[:, :n], in0=H2[:, :n], scalar1=CHI,
                 scalar2=None, op0=OP.mult)
            utgt = v_op("scalar_tensor_tensor", sem=sem_u, out=F2[:, :n],
                        in0=H2[:, :n], scalar=CLO, in1=F1[:, :n],
                        op0=OP.mult, op1=OP.add)
            wait("S", sem_u, utgt)
            s_op("activation", out=B1[:, :n], in_=F2[:, :n], func=AF.Exp,
                 scale=1.0, bias=b0c[:])
            etgt = s_op("activation", sem=sS, out=B2[:, :n], in_=F2[:, :n],
                        func=AF.Exp, scale=-1.0, bias=b0c[:])
            wait("V", sS, etgt)
            return etgt

        def chain_m_unsigned(r_src, n, q_out, sem):
            v_op("tensor_tensor", out=H3[:, :n], in0=r_src, in1=B1[:, :n], op=OP.mult)
            v_op("tensor_copy", out=I0[:, :n], in_=H3[:, :n])
            v_op("tensor_copy", out=H1[:, :n], in_=I0[:, :n])
            v_op("tensor_tensor", out=H2[:, :n], in0=H3[:, :n], in1=H1[:, :n],
                 op=OP.subtract)
            v_op("scalar_tensor_tensor", out=H0[:, :n], in0=H2[:, :n], scalar=0.5,
                 in1=H1[:, :n], op0=OP.is_gt, op1=OP.add)
            v_op("scalar_tensor_tensor", out=H1[:, :n], in0=H2[:, :n], scalar=-0.5,
                 in1=H0[:, :n], op0=OP.is_ge, op1=OP.add)
            return v_op("scalar_tensor_tensor", sem=sem, out=q_out, in0=H1[:, :n],
                        scalar=-1.0, in1=B2[:, :n], op0=OP.add, op1=OP.mult)

        def chain_m_signed(x_src, n, q_out, sem):
            v_op("tensor_tensor", out=F1[:, :n], in0=x_src, in1=B1[:, :n], op=OP.mult)
            v_op("tensor_copy", out=I0[:, :n], in_=F1[:, :n])
            v_op("tensor_copy", out=H1[:, :n], in_=I0[:, :n])
            v_op("tensor_tensor", out=F2[:, :n], in0=F1[:, :n], in1=H1[:, :n],
                 op=OP.subtract)
            v_op("scalar_tensor_tensor", out=H0[:, :n], in0=F2[:, :n], scalar=0.5,
                 in1=H1[:, :n], op0=OP.is_gt, op1=OP.add)
            v_op("scalar_tensor_tensor", out=H1[:, :n], in0=F2[:, :n], scalar=-0.5,
                 in1=H0[:, :n], op0=OP.is_ge, op1=OP.add)
            return v_op("scalar_tensor_tensor", sem=sem, out=q_out, in0=H1[:, :n],
                        scalar=-1.0, in1=B2[:, :n], op0=OP.add, op1=OP.mult)

        def getsl(src, c, qb):
            return src[:, c * S + qb * NQ: c * S + qb * NQ + NQ]

        def ln_stats(src, qb):
            wait("S", sT, sT.c)
            for c in range(DCH):
                s_op("activation", sem=sS if c == DCH - 1 else None,
                     out=xsq[:, c * NQ:(c + 1) * NQ], in_=getsl(src, c, qb),
                     func=AF.Square, scale=1.0, bias=b0c[:])
            wait("T", sS, sS.c)
            tgt = None
            for c in range(DCH):
                t_op("matmul", out=psum[6][:], lhsT=c1024b[:], rhs=getsl(src, c, qb),
                     start=(c == 0), stop=(c == DCH - 1))
                tgt = t_op("matmul", sem=sT if c == DCH - 1 else None,
                           out=psum[7][:], lhsT=c1024b[:],
                           rhs=xsq[:, c * NQ:(c + 1) * NQ],
                           start=(c == 0), stop=(c == DCH - 1))
            return tgt

        def ln_statmath(composed, t_tgt):
            wait("V", sT, t_tgt)
            v_op("tensor_copy", out=mu_s[:], in_=psum[6][:])
            v_op("tensor_copy", out=m2_s[:], in_=psum[7][:])
            v_op("tensor_tensor", out=t0_s[:], in0=mu_s[:], in1=mu_s[:], op=OP.mult)
            vtgt = v_op("tensor_tensor", sem=sV, out=t1_s[:], in0=m2_s[:],
                        in1=t0_s[:], op=OP.subtract)
            wait("S", sV, vtgt)
            s_op("activation", out=t0_s[:], in_=t1_s[:], func=AF.Ln, scale=1.0,
                 bias=epsc[:])
            stgt = s_op("activation", sem=sS, out=rr_s[:], in_=t0_s[:], func=AF.Exp,
                        scale=-0.5, bias=b0c[:])
            wait("V", sS, stgt)
            if composed:
                v_op("tensor_scalar", out=t0_s[:], in0=t1_s[:], scalar1=float(EPS),
                     scalar2=None, op0=OP.add)
                v_op("reciprocal", out=t0_s[:], in_=t0_s[:])
                vtgt2 = v_op("tensor_tensor", sem=sV, out=t1_s[:], in0=t1_s[:],
                             in1=t0_s[:], op=OP.mult)
                wait("S", sV, vtgt2)
                s_op("activation", out=t0_s[:], in_=t1_s[:], func=AF.Ln, scale=1.0,
                     bias=epsc[:])
                stgt2 = s_op("activation", sem=sS, out=t1_s[:], in_=t0_s[:],
                             func=AF.Exp, scale=-0.5, bias=b0c[:])
                wait("V", sS, stgt2)
                v_op("tensor_tensor", out=rr_s[:], in0=rr_s[:], in1=t1_s[:],
                     op=OP.mult)

        def ln_normalize(src, qb, dst, g=None, b=None):
            for c in range(DCH):
                sl = slice(c * S + qb * NQ, c * S + qb * NQ + NQ)
                v_op("scalar_tensor_tensor", out=F1[:, :NQ], in0=src[:, sl],
                     scalar=1.0, in1=mu_s[:], op0=OP.mult, op1=OP.subtract)
                last = (c == DCH - 1)
                if g is None:
                    v_op("tensor_tensor", sem=sV if last else None,
                         out=dst[:, sl], in0=F1[:, :NQ], in1=rr_s[:], op=OP.mult)
                else:
                    v_op("tensor_tensor", out=F1[:, :NQ], in0=F1[:, :NQ],
                         in1=rr_s[:], op=OP.mult)
                    v_op("tensor_scalar", sem=sV if last else None,
                         out=dst[:, sl], in0=F1[:, :NQ],
                         scalar1=g[:, c:c + 1], scalar2=b[:, c:c + 1],
                         op0=OP.mult, op1=OP.add)

        def precast(slabs):
            fbufs = [F3, F2]
            bbufs = [B0, B2]
            cp = [0, 0]
            st = [0, 0]
            for i, (src, dst, w) in enumerate(slabs):
                j = i % 2
                wait("Y", sS, cp[j])
                ld = dma("Y", fbufs[j][:, :w], src, dS)
                wait("S", dS, ld)
                wait("S", dS, st[j])
                cp[j] = s_op("activation", sem=sS, out=bbufs[j][:, :w],
                             in_=fbufs[j][:, :w], func=AF.Copy, scale=1.0,
                             bias=0.0)
                wait("Y", sS, cp[j])
                st[j] = dma("Y", dst, bbufs[j][:, :w], dS)
            return max(st)

        def slabs_of(src, dst, rows, cols):
            out = []
            for r0 in range(0, rows, P):
                for c0 in range(0, cols, HALF):
                    w = min(HALF, cols - c0)
                    out.append((src[r0:r0 + P, c0:c0 + w],
                                dst[r0:r0 + P, c0:c0 + w], w))
            return out

        wcast_tgt = [0]
        pb_cast = precast(slabs_of(pbw1, pbw1b, D, D)
                          + slabs_of(pbw2, pbw2b, D, HEADS))

        wait("T", dG, ld_g)
        wait("T", sV, init_v2)
        h0 = arena
        H1OFF = DCH * S

        for c in range(DCH):
            for qb in range(NQB):
                bank = (c * NQB + qb) % 4
                if c * NQB + qb >= 4:
                    wait("T", sS, sS.c)
                ttgt = t_op("matmul", sem=sT, out=psum[bank][:],
                            lhsT=pw0b0S[:, c * P:(c + 1) * P],
                            rhs=kvT[0:2, qb * NQ:(qb + 1) * NQ],
                            start=True, stop=True)
                wait("S", sT, ttgt)
                s_op("activation", sem=sS, out=getsl(h0, c, qb),
                     in_=psum[bank][:], func=AF.Copy, scale=1.0, bias=0.0)
        wait("T", sS, sS.c)
        wait("V", sS, sS.c)

        class OffT:
            def __init__(self, t, off):
                self.t, self.off = t, off
            def __getitem__(self, key):
                p, sl = key
                return self.t[p, slice(sl.start + self.off, sl.stop + self.off)]
        OffT2 = OffT

        def posmlp_ln_silu(src, gS, bS):
            for qb in range(NQB):
                ttgt = ln_stats(src, qb)
                ln_statmath(False, ttgt)
                for c in range(DCH):
                    sl = slice(c * S + qb * NQ, c * S + qb * NQ + NQ)
                    v_op("scalar_tensor_tensor", out=F1[:, :NQ], in0=src[:, sl],
                         scalar=1.0, in1=mu_s[:], op0=OP.mult, op1=OP.subtract)
                    v_op("tensor_tensor", out=F1[:, :NQ], in0=F1[:, :NQ],
                         in1=rr_s[:], op=OP.mult)
                    vtgt = v_op("tensor_scalar", sem=sV, out=F2[:, :NQ],
                                in0=F1[:, :NQ], scalar1=gS[:, c:c + 1],
                                scalar2=bS[:, c:c + 1], op0=OP.mult, op1=OP.add)
                    wait("S", sV, vtgt)
                    stgt = s_op("activation", sem=sS, out=F1[:, :NQ], in_=F2[:, :NQ],
                                func=AF.Exp, scale=-1.0, bias=b0c[:])
                    wait("V", sS, stgt)
                    v_op("tensor_scalar", out=F1[:, :NQ], in0=F1[:, :NQ],
                         scalar1=1.0, scalar2=None, op0=OP.add)
                    v_op("reciprocal", out=F1[:, :NQ], in_=F1[:, :NQ])
                    v_op("tensor_tensor", sem=sV, out=xn[:, sl], in0=F2[:, :NQ],
                         in1=F1[:, :NQ], op=OP.mult)
            wait("T", sV, sV.c)

        dump_sb("h0d", arena, DCH * S)
        posmlp_ln_silu(h0, pl0gS, pl0bS)
        dump_sb("pm0", xn, DCH * S)

        wtgts = []
        wait("Y", sT, sT.c)
        wait("Y", dS, pb_cast)
        for c2 in range(DCH):
            for fc in range(DCH):
                buf = wb[(c2 * DCH + fc) % 2]
                k = c2 * DCH + fc
                if k >= 2:
                    wait("Y", sT, wtgts[k - 2])
                dtgt = dma("Y", buf[:, 0:P],
                           pbw1b[fc * P:(fc + 1) * P, c2 * P:(c2 + 1) * P], dS)
                wait("T", dS, dtgt)
                for qb in range(NQB):
                    t = t_op("matmul",
                             sem=sT if qb == NQB - 1 else None,
                             out=psum[qb][:], lhsT=buf[:, 0:P],
                             rhs=getsl(xn, fc, qb),
                             start=(fc == 0), stop=(fc == DCH - 1))
                wtgts.append(sT.c)
            wait("S", sT, sT.c)
            for qb in range(NQB):
                s_op("activation", sem=sS if qb == NQB - 1 else None,
                     out=arena[:, H1OFF + c2 * S + qb * NQ: H1OFF + c2 * S + qb * NQ + NQ],
                     in_=psum[qb][:], func=AF.Identity, scale=1.0,
                     bias=pb1S[:, c2:c2 + 1])
            wait("T", sS, sS.c)

        dump_sb("h1r", arena, 2 * DCH * S)
        posmlp_ln_silu(OffT(arena, H1OFF), pl1gS, pl1bS)
        dump_sb("pm1", xn, DCH * S)

        wtgts = []
        wait("Y", sT, sT.c)
        for qb in range(NQB):
            for fc in range(DCH):
                buf = wb[(qb * DCH + fc) % 2]
                k = qb * DCH + fc
                if k >= 2:
                    wait("Y", sT, wtgts[k - 2])
                dtgt = dma("Y", buf[:, 0:HEADS], pbw2b[fc * P:(fc + 1) * P, :], dS)
                wait("T", dS, dtgt)
                ttgt = t_op("matmul", sem=sT,
                            out=psum[4][0:HEADS, :], lhsT=buf[:, 0:HEADS],
                            rhs=getsl(xn, fc, qb),
                            start=(fc == 0), stop=(fc == DCH - 1))
                wtgts.append(ttgt)
            wait("S", sT, sT.c)
            s_op("activation", sem=sS, out=kkB[0:HEADS, qb * NQ:(qb + 1) * NQ],
                 in_=psum[4][0:HEADS, :], func=AF.Exp, scale=1.0, bias=pb2S[:])
            wait("T", sS, sS.c)
        eb_done = sS.c
        wait("Y", sS, eb_done)
        wait("Y", sV, init_v2)
        dma("Y", ebtab[:, 0:EBPAD], kvT[32:48, 0:EBPAD], dS)
        ebw_tgt = dma("Y", ebtab[:, EBPAD:EBW], kkB[0:HEADS, :], dS)

        n8 = P
        wait("S", sV, init_v2)
        v_op("memset", ap=t1_s[:, :n8], constant=1.0)
        padtgt = v_op("tensor_copy", sem=sV, out=t1_s[:, :DCH], in_=b2S[:])
        wait("S", sV, padtgt)
        atgt = s_op("activation", sem=sS, out=B0[:, :n8], in_=t1_s[:, :n8],
                    func=AF.Abs, scale=1.0, bias=b0c[:])
        wait("V", sS, atgt)
        pretgt = v_op("tensor_scalar", sem=sV, out=B1[:, :n8], in0=B0[:, :n8],
                      scalar1=2.0 ** -14, scalar2=None, op0=OP.max)
        wait("S", sV, pretgt)
        chain_core(n8, float(2.0 ** -10), sV)
        chain_m_signed(t1_s[:, :n8], n8, t0_s[:, :n8], None)
        bq_tgt = v_op("tensor_copy", sem=sV, out=bqS[:], in_=t0_s[:, :DCH])

        dump_sb("bqv", bqS, DCH, dt=F32)
        for ch in range(HCH):
            dtgt = dma("Y", F3[:], w2T[ch * P:(ch + 1) * P, :], dS)
            wait("S", dS, dtgt)
            n = HALF
            atgt = s_op("activation", sem=sS, out=B0[:, :n], in_=F3[:],
                        func=AF.Abs, scale=1.0, bias=b0c[:])
            wait("V", sS, atgt)
            pretgt = v_op("tensor_scalar", sem=sV, out=B1[:, :n], in0=B0[:, :n],
                          scalar1=2.0 ** -6, scalar2=None, op0=OP.max)
            wait("S", sV, pretgt)
            chain_core(n, 0.125, sV)
            qtgt = chain_m_signed(F3[:], n, B0[:, :n], sV)
            wait("G", sV, qtgt)
            wtg = dma("G", wqTd[ch * P:(ch + 1) * P, :], B0[:, :n], dG)
            wait("Y", sV, qtgt)
            wait("S", dG, wtg)
        wq_done_g = dG.c
        if "wq" in dbg:
            t_wq = nc.dram_tensor("dbg_wq", [HF, D], BF16,
                                  kind="ExternalOutput").ap()
            wait("Y", dG, wq_done_g)
            dma("Y", t_wq, wqTd, dS)

        wait("Y", sV, sV.c)
        wait("S", sV, sV.c)
        wait("S", dG, dG.c)
        wcast_tgt[0] = precast(
            slabs_of(w1T, w1Tb, D, HF) + slabs_of(Wq, Wqb, D, D)
            + slabs_of(Wkv, Wkvb, D, 2 * HD) + slabs_of(Wo, Wob, D, D))

        def ffn_call(tag=""):
            wait("S", sV, sV.c)
            for qb in range(NQB):
                ttgt = ln_stats(x, qb)
                ln_statmath(False, ttgt)
                ln_normalize(x, qb, xn)
            wait("T", sV, sV.c)
            dump_sb("xn_" + tag, xn, DCH * S)
            for half in range(2):
                off = half * HALF
                relu_tgts = []
                q_tgts = []
                wtgts = []
                wait("Y", sT, sT.c)
                wait("Y", dS, wcast_tgt[0])
                for hb in range(HCH):
                    pp = (hb % 2) * 2
                    if hb >= 2:
                        wait("T", sS, relu_tgts[hb - 2])
                    for g in range(2):
                        kg = hb * 2 + g
                        buf = wb[kg % 2]
                        if kg >= 2:
                            wait("Y", sT, wtgts[kg - 2])
                        src = w1Tb.copy()
                        src.ap = bass_rust.VecI64Pair(
                            [(HF, P), (P * HF, 4), (1, P)])
                        src.offset = g * 4 * P * HF + hb * P
                        dtgt = dma("Y", buf[:, 0:4 * P], src, dS)
                        wait("T", dS, dtgt)
                        for fi in range(4):
                            fc = g * 4 + fi
                            for hh in range(2):
                                t_op("matmul",
                                     sem=sT if (hh == 1 and fi == 3) else None,
                                     out=psum[pp + hh][:],
                                     lhsT=buf[:, fi * P:(fi + 1) * P],
                                     rhs=xn[:, fc * S + off + hh * NQ: fc * S + off + hh * NQ + NQ],
                                     start=(fc == 0), stop=(fc == DCH - 1))
                        wtgts.append(sT.c)
                    wait("S", sT, sT.c)
                    if hb >= 2:
                        wait("S", sV, q_tgts[hb - 2])
                    rb = B0 if hb % 2 == 0 else B3
                    s_op("activation", out=rb[:, 0:NQ], in_=psum[pp][:],
                         func=AF.Relu, scale=1.0, bias=b1S[:, hb:hb + 1])
                    rtgt = s_op("activation", sem=sS, out=rb[:, NQ:HALF],
                                in_=psum[pp + 1][:], func=AF.Relu, scale=1.0,
                                bias=b1S[:, hb:hb + 1])
                    relu_tgts.append(rtgt)
                    wait("V", sS, rtgt)
                    pretgt = v_op("tensor_scalar", sem=sV, out=B1[:, :HALF],
                                  in0=rb[:, :HALF], scalar1=2.0 ** -6,
                                  scalar2=None, op0=OP.max)
                    wait("S", sV, pretgt)
                    chain_core(HALF, 0.125, sV)
                    q_tgts.append(chain_m_unsigned(
                        rb[:, :HALF], HALF,
                        arena[:, hb * HALF:(hb + 1) * HALF], sV))
                wait("T", sV, sV.c)
                dump_sb("hq_%s_%d" % (tag, half), arena, HCH * HALF)
                out_tgts = []
                res_tgts = []
                wtgts = []
                wait("Y", sT, sT.c)
                wait("Y", dG, wq_done_g)
                for db in range(DCH):
                    pp = 4 + (db % 2) * 2
                    if db >= 2:
                        wait("T", sS, out_tgts[db - 2])
                    for g in range(8):
                        kg = db * 8 + g
                        buf = wb[kg % 2]
                        if kg >= 2:
                            wait("Y", sT, wtgts[kg - 2])
                        src = wqTd.copy()
                        src.ap = bass_rust.VecI64Pair(
                            [(D, P), (P * D, 4), (1, P)])
                        src.offset = g * 4 * P * D + db * P
                        dtgt = dma("Y", buf[:, 0:4 * P], src, dS)
                        wait("T", dS, dtgt)
                        for hi in range(4):
                            hb = g * 4 + hi
                            for hh in range(2):
                                t_op("matmul",
                                     sem=sT if (hh == 1 and hi == 3) else None,
                                     out=psum[pp + hh][:],
                                     lhsT=buf[:, hi * P:(hi + 1) * P],
                                     rhs=arena[:, hb * HALF + hh * NQ: hb * HALF + hh * NQ + NQ],
                                     start=(hb == 0), stop=(hb == HCH - 1))
                        wtgts.append(sT.c)
                    wait("S", sT, sT.c)
                    if db >= 1:
                        wait("S", sV, res_tgts[db - 1])
                    s_op("activation", out=F3[:, 0:NQ], in_=psum[pp][:],
                         func=AF.Identity, scale=1.0, bias=bqS[:, db:db + 1])
                    s_op("activation", out=F3[:, NQ:HALF], in_=psum[pp + 1][:],
                         func=AF.Identity, scale=1.0, bias=bqS[:, db:db + 1])
                    atgt = s_op("activation", sem=sS, out=B0[:, :HALF],
                                in_=F3[:, :HALF], func=AF.Abs, scale=1.0, bias=b0c[:])
                    out_tgts.append(atgt)
                    shad = ("shad" in dbg and tag == "f1" and half == 1
                            and db == 1)
                    if shad:
                        s_op("activation", out=peA[:], in_=psum[pp + 1][:],
                             func=AF.Copy, scale=1.0, bias=0.0)
                    if db == 0 and half == 0:
                        dump_sb("f3_" + tag, F3, HALF, dt=F32)
                    wait("V", sS, atgt)
                    pretgt = v_op("tensor_scalar", sem=sV, out=B1[:, :HALF],
                                  in0=B0[:, :HALF], scalar1=2.0 ** -14,
                                  scalar2=None, op0=OP.max)
                    wait("S", sV, pretgt)
                    chain_core(HALF, float(2.0 ** -10), sV)
                    chain_m_signed(F3[:, :HALF], HALF, F0[:, :HALF], None)
                    if shad:
                        v_op("tensor_copy", out=peB[:], in_=F3[:, NQ:HALF])
                        v_op("tensor_copy", out=pB[:], in_=F0[:, NQ:HALF])
                    for hh in range(2):
                        sl = slice(db * S + off + hh * NQ, db * S + off + hh * NQ + NQ)
                        rt = v_op("scalar_tensor_tensor",
                                  sem=sV if hh == 1 else None,
                                  out=x[:, sl], in0=F0[:, hh * NQ:(hh + 1) * NQ],
                                  scalar=0.5, in1=x[:, sl], op0=OP.mult, op1=OP.add)
                    res_tgts.append(rt)
                wait("T", sV, sV.c)
            if "shad" in dbg and tag == "f1":
                dump_sb("shA", peA, NQ)
                dump_sb("shB", peB, NQ)
                dump_sb("shQ", pB, NQ)

        def attention():
            wait("S", sV, sV.c)
            for qb in range(NQB):
                ttgt = ln_stats(x, qb)
                ln_statmath(True, ttgt)
                ln_normalize(x, qb, xn, g=lngS, b=lnbS)
            wait("T", sV, sV.c)
            dump_sb("xn_at", xn, DCH * S)
            wtgts = []
            wait("Y", sT, sT.c)
            for qc in range(DCH):
                for g in range(2):
                    kg = qc * 2 + g
                    buf = wb[kg % 2]
                    if kg >= 2:
                        wait("Y", sT, wtgts[kg - 2])
                    src = Wqb.copy()
                    src.ap = bass_rust.VecI64Pair([(D, P), (P * D, 4), (1, P)])
                    src.offset = g * 4 * P * D + qc * P
                    dtgt = dma("Y", buf[:, 0:4 * P], src, dS)
                    wait("T", dS, dtgt)
                    for fi in range(4):
                        fc = g * 4 + fi
                        for qb in range(NQB):
                            t_op("matmul",
                                 sem=sT if (qb == NQB - 1 and fi == 3) else None,
                                 out=psum[qb][:],
                                 lhsT=buf[:, fi * P:(fi + 1) * P],
                                 rhs=getsl(xn, fc, qb),
                                 start=(fc == 0), stop=(fc == DCH - 1))
                    wtgts.append(sT.c)
                wait("S", sT, sT.c)
                for qb in range(NQB):
                    s_op("activation", sem=sS if qb == NQB - 1 else None,
                         out=arena[:, qc * S + qb * NQ: qc * S + qb * NQ + NQ],
                         in_=psum[qb][:], func=AF.Copy, scale=0.125, bias=0.0)
                wait("T", sS, sS.c)
            wtgts = []
            wait("Y", sT, sT.c)
            for g in range(2):
                buf = wb[g % 2]
                src = Wkvb.copy()
                src.ap = bass_rust.VecI64Pair(
                    [(2 * HD, P), (P * 2 * HD, 4), (1, 2 * HD)])
                src.offset = g * 4 * P * 2 * HD
                dtgt = dma("Y", buf[:, 0:4 * 2 * HD], src, dS)
                wait("T", dS, dtgt)
                for fi in range(4):
                    fc = g * 4 + fi
                    for qb in range(NQB):
                        t_op("matmul",
                             sem=sT if (qb == NQB - 1 and fi == 3) else None,
                             out=psum[4 + qb][:],
                             lhsT=buf[:, fi * 2 * HD:(fi + 1) * 2 * HD],
                             rhs=getsl(xn, fc, qb),
                             start=(fc == 0), stop=(fc == DCH - 1))
                wtgts.append(sT.c)
            wait("V", sT, sT.c)
            wait("V", dS, ebw_tgt)
            for qb in range(NQB):
                for blk in range(NQ // 128):
                    col0 = qb * NQ + blk * 128
                    rev = kvT[:, col0:col0 + 128].copy()
                    pitch = kvT.ap().ap[0][0]
                    rev.ap = bass_rust.VecI64Pair([(pitch, P), (-1, 128)])
                    rev.offset = col0 + 127
                    v_op("tensor_copy",
                         sem=sV if (qb == NQB - 1 and blk == 3) else None,
                         out=rev, in_=psum[4 + qb][:, blk * 128:(blk + 1) * 128])
            kv_done = sV.c
            wait("Y", sV, kv_done)
            dma("Y", kkB[HD:P, :], kvT[0:HD, :], dS)
            kv_aux = dS.c
            wait("Y", sS, sS.c)
            wait("Y", sV, sV.c)
            for fc in range(DCH):
                wkv_tgt = dma("Y", B1[:, fc * HD:(fc + 1) * HD],
                              Wkvb[fc * P:(fc + 1) * P, HD:2 * HD], dS)
            wait("T", dS, wkv_tgt)
            rv_tgts = []
            mm_tgts = []
            ev_tgts = []
            for kb in range(16):
                dst = B0 if kb % 2 == 0 else B2
                dpitch = dst.ap().ap[0][0]
                if kb >= 2:
                    wait("V", sT, mm_tgts[kb - 2])
                for c in range(DCH):
                    rev = dst[:, c * P:(c + 1) * P].copy()
                    rev.ap = bass_rust.VecI64Pair([(dpitch, P), (-1, P)])
                    rev.offset = c * P + 127
                    rv = v_op("tensor_copy", sem=sV if c == DCH - 1 else None,
                              out=rev,
                              in_=xn[:, c * S + kb * P: c * S + (kb + 1) * P])
                rv_tgts.append(rv)
                wait("T", sV, rv_tgts[kb])
                if kb >= 2:
                    wait("T", sS, ev_tgts[kb - 2])
                for c in range(DCH):
                    mm = t_op("matmul", sem=sT if c == DCH - 1 else None,
                              out=psum[kb % 2][:, 0:HD],
                              lhsT=dst[:, c * P:(c + 1) * P],
                              rhs=B1[:, c * HD:(c + 1) * HD],
                              start=(c == 0), stop=(c == DCH - 1))
                mm_tgts.append(mm)
                wait("S", sT, mm_tgts[kb])
                ev_tgts.append(s_op("activation", sem=sS,
                                    out=vblk[:, kb * 65:kb * 65 + HD],
                                    in_=psum[kb % 2][:, 0:HD],
                                    func=AF.Copy, scale=1.0, bias=0.0))
            wait("T", sS, ev_tgts[-1])
            wait("T", dS, kv_aux)
            dump_sb("q", arena, DCH * S)
            dump_sb("kvT", kvT, S)
            dump_sb("vblk", vblk, 17 * 65)
            wait("Y", dS, kv_aux)
            dump_sb("kkBd", kkB, S)
            dump_sb("knull2d", knull2, P)
            WOC = 20480
            wait("Y", sT, sT.c)
            for c in range(DCH):
                dma("Y", arena[:, WOC + c * D: WOC + (c + 1) * D],
                    Wob[c * P:(c + 1) * P, :], dS)
            wo_tgt = dS.c
            wait("T", dS, wo_tgt)
            AVQ = 16384

            eb_targets = {}
            fix_prev = None
            pe_prev = None
            for qb in range(NQB):
                i0 = qb * NQ
                kbs = list(range(4 * qb + 4)) + [16]
                for c in range(DCH):
                    hA, hB = 2 * c, 2 * c + 1
                    exp_tgts = []
                    pe_tgts = []
                    av_tgts = []
                    for ki, kb in enumerate(kbs):
                        last = (ki == len(kbs) - 1)
                        ebuf = ki % 2
                        if kb < 16:
                            if ki >= 2:
                                wait("Y", sV, pe_tgts[ki - 2])
                                wait("G", sV, pe_tgts[ki - 2])
                            elif pe_prev is not None:
                                wait("Y", sV, pe_prev)
                                wait("G", sV, pe_prev)
                            j0 = kb * 128
                            offA = hA * EBW + EBPAD + i0 - j0 - 127
                            offB = hB * EBW + EBPAD + i0 - j0 - 127
                            eA = dma("Y", ebtA[ebuf][:], toep(offA), dS)
                            eB = dma("G", ebtB[ebuf][:], toep(offB), dG)
                            lA = kvT[0:HD, j0:j0 + 128]
                            lB = kkB[HD:P, j0:j0 + 128]
                        else:
                            eA = eB = None
                            lA = knull2[0:HD, :]
                            lB = knull2[HD:P, :]
                        if ki >= 1:
                            wait("T", sS, exp_tgts[ki - 1])
                            wait("S", sV, pe_tgts[ki - 1])
                        t_op("matmul", out=psum[0][:], lhsT=lA,
                             rhs=arena[0:HD, c * S + i0: c * S + i0 + NQ],
                             start=True, stop=True)
                        ttgt = t_op("matmul", sem=sT, out=psum[1][:], lhsT=lB,
                                    rhs=arena[HD:P, c * S + i0: c * S + i0 + NQ],
                                    start=True, stop=True)
                        wait("S", sT, ttgt)
                        s_op("activation", out=pA[:], in_=psum[0][:], func=AF.Exp,
                             scale=1.0, bias=b0c[:])
                        etg = s_op("activation", sem=sS, out=pB[:], in_=psum[1][:],
                                   func=AF.Exp, scale=1.0, bias=b0c[:])
                        exp_tgts.append(etg)
                        wait("V", sS, etg)
                        if ki >= 1:
                            wait("V", sT, av_tgts[ki - 1])
                        if kb < 16:
                            wait("V", dS, eA)
                            wait("V", dG, eB)
                            v_op("tensor_tensor", out=peA[:], in0=pA[:],
                                 in1=ebtA[ebuf][:], op=OP.mult)
                            vtg = v_op("tensor_tensor", sem=sV, out=peB[:],
                                       in0=pB[:], in1=ebtB[ebuf][:], op=OP.mult)
                        else:
                            v_op("tensor_scalar", out=peA[:], in0=pA[:],
                                 scalar1=nullm[:, hA:hA + 1], scalar2=None,
                                 op0=OP.mult)
                            vtg = v_op("tensor_scalar", sem=sV, out=peB[:],
                                       in0=pB[:], scalar1=nullm[:, hB:hB + 1],
                                       scalar2=None, op0=OP.mult)
                        pe_tgts.append(vtg)
                        pe_prev = vtg
                        wait("T", sV, vtg)
                        t_op("matmul", out=psum[2][0:65, :],
                             lhsT=vblk[:, kb * 65:(kb + 1) * 65], rhs=peA[:],
                             start=(ki == 0), stop=last)
                        av_tgts.append(t_op(
                            "matmul", sem=sT,
                            out=psum[3][0:65, :],
                            lhsT=vblk[:, kb * 65:(kb + 1) * 65], rhs=peB[:],
                            start=(ki == 0), stop=last))
                    av_tgt = av_tgts[-1]
                    wait("V", sT, av_tgt)
                    wait("S", sT, av_tgt)
                    vr = v_op("reciprocal", sem=sV, out=t0_s[0:1, :],
                              in_=psum[2][HD:HD + 1, :])
                    wait("T", sV, vr)
                    rA = t_op("matmul", sem=sT, out=psum[4][0:HD, :], lhsT=onesrF[:],
                              rhs=t0_s[0:1, :], start=True, stop=True)
                    sA = s_op("activation", sem=sS, out=tav[:], in_=psum[2][0:HD, :],
                              func=AF.Copy, scale=1.0, bias=0.0)
                    wait("V", sT, rA)
                    wait("V", sS, sA)
                    avnA = v_op("tensor_tensor", sem=sV,
                                out=arena[0:HD, AVQ + c * NQ: AVQ + (c + 1) * NQ],
                                in0=tav[:], in1=psum[4][0:HD, :], op=OP.mult)
                    vr = v_op("reciprocal", sem=sV, out=t0_s[0:1, :],
                              in_=psum[3][HD:HD + 1, :])
                    wait("T", sV, vr)
                    rB = t_op("matmul", sem=sT, out=psum[5][0:HD, :], lhsT=onesrF[:],
                              rhs=t0_s[0:1, :], start=True, stop=True)
                    wait("S", sV, avnA)
                    sB = s_op("activation", sem=sS, out=tav[:], in_=psum[3][0:HD, :],
                              func=AF.Copy, scale=1.0, bias=0.0)
                    wait("V", sT, rB)
                    wait("V", sS, sB)
                    if fix_prev is not None:
                        wait("V", dG, fix_prev)
                    vB = v_op("tensor_tensor", sem=sV, out=tavn[:], in0=tav[:],
                              in1=psum[5][0:HD, :], op=OP.mult)
                    wait("G", sV, vB)
                    fix = dma("G", arena[HD:P, AVQ + c * NQ: AVQ + (c + 1) * NQ],
                              tavn[:], dG)
                    fix_prev = fix
                    wait("T", sS, sB)
                    wait("T", sV, vB)
                dump_sb("av_q%d" % qb, OffT2(arena, AVQ), DCH * NQ)
                wait("T", dG, dG.c)
                o_tgts = []
                for db in range(DCH):
                    bank = 6 + db % 2
                    if db >= 2:
                        wait("T", sS, o_tgts[db - 2])
                    for cc in range(DCH):
                        t_op("matmul", sem=sT if cc == DCH - 1 else None,
                             out=psum[bank][:],
                             lhsT=arena[:, WOC + cc * D + db * P: WOC + cc * D + (db + 1) * P],
                             rhs=arena[:, AVQ + cc * NQ: AVQ + (cc + 1) * NQ],
                             start=(cc == 0), stop=(cc == DCH - 1))
                    wait("S", sT, sT.c)
                    so = s_op("activation", sem=sS,
                              out=xn[:, db * S + i0: db * S + i0 + NQ],
                              in_=psum[bank][:], func=AF.Copy, scale=1.0, bias=0.0)
                    o_tgts.append(so)
                wait("V", sT, sT.c)
                wait("G", sT, sT.c)
            o_done = sS.c
            wait("V", sS, o_done)
            dump_sb("ao", xn, DCH * S)
            for qb in range(NQB):
                ttgt = ln_stats(xn, qb)
                ln_statmath(False, ttgt)
                for c in range(DCH):
                    sl = slice(c * S + qb * NQ, c * S + qb * NQ + NQ)
                    v_op("scalar_tensor_tensor", out=F1[:, :NQ], in0=xn[:, sl],
                         scalar=1.0, in1=mu_s[:], op0=OP.mult, op1=OP.subtract)
                    v_op("tensor_tensor", out=F1[:, :NQ], in0=F1[:, :NQ],
                         in1=rr_s[:], op=OP.mult)
                    v_op("tensor_scalar", out=F2[:, :NQ], in0=F1[:, :NQ],
                         scalar1=olngS[:, c:c + 1], scalar2=olnbS[:, c:c + 1],
                         op0=OP.mult, op1=OP.add)
                    v_op("tensor_tensor", sem=sV if c == DCH - 1 else None,
                         out=x[:, sl], in0=x[:, sl], in1=F2[:, :NQ], op=OP.add)

        wait("T", dG, wq_done_g)
        wait("T", sV, bq_tgt)
        wait("S", sV, bq_tgt)
        if "eb" in dbg:
            t_eb = nc.dram_tensor("dbg_eb", [HEADS, EBW], BF16,
                                  kind="ExternalOutput").ap()
            wait("Y", dS, ebw_tgt)
            dma("Y", t_eb, ebtab, dS)
        ffn_call("f1")
        if "sh19" in dbg:
            dump_sb("s19a", mu_s, NQ, dt=F32)
            dump_sb("s19b", m2_s, NQ, dt=F32)
            dump_sb("s19h0", pA, NQ)
            dump_sb("s19h1", pB, NQ)
            if "w19" in dbg:
                t19 = nc.dram_tensor("dbg_w19", [D, P], BF16,
                                     kind="ExternalOutput").ap()
                for rr in range(DCH):
                    ld19 = dma("Y", peA[:, 0:P],
                               w1Tb[rr * P:(rr + 1) * P, 19 * P:20 * P], dS)
                    wait("Y", dS, ld19)
                    dma("Y", t19[rr * P:(rr + 1) * P, :], peA[:, 0:P], dS)
                wait("V", dS, dS.c)
        dump_x("x1")
        wait("Y", sV, sV.c)
        wait("Y", dS, ebw_tgt)
        attention()
        dump_x("x2")
        ffn_call("f3")

        obufs = [F0, F1]
        otgts = [0, 0]
        for k in range(2 * DCH):
            c, h = k // 2, k % 2
            buf = obufs[k % 2]
            wait("V", dS, otgts[k % 2])
            vtgt = v_op("tensor_copy", sem=sV, out=buf[:, :HALF],
                        in_=x[:, c * S + h * HALF: c * S + (h + 1) * HALF])
            wait("Y", sV, vtgt)
            otgts[k % 2] = dma("Y", o_out[c * P:(c + 1) * P, h * HALF:(h + 1) * HALF],
                               buf[:, :HALF], dS)
        wait("Y", dS, dS.c)

        block = es.enter_context(nc.Block())

        @block.sync
        def _(sync):
            for fn in plan["Y"]:
                fn(sync)

        @block.gpsimd
        def _(g):
            for fn in plan["G"]:
                fn(g)

        @block.tensor
        def _(t):
            for fn in plan["T"]:
                fn(t)

        @block.vector
        def _(v):
            for fn in plan["V"]:
                fn(v)

        @block.scalar
        def _(s):
            for fn in plan["S"]:
                fn(s)

    return nc


def prep_base(L0_attn_ln_g, L0_attn_ln_b, L0_attn_Wq, L0_attn_Wkv,
              L0_attn_null_kv, L0_attn_null_attn_bias, L0_attn_pb_w0,
              L0_attn_pb_b0, L0_attn_pb_ln0_g, L0_attn_pb_ln0_b, L0_attn_pb_w1,
              L0_attn_pb_b1, L0_attn_pb_ln1_g, L0_attn_pb_ln1_b, L0_attn_pb_w2,
              L0_attn_pb_b2, L0_attn_Wo, L0_attn_out_ln_g, L0_attn_out_ln_b,
              L0_ffn_w1, L0_ffn_b1, L0_ffn_w2, L0_ffn_b2, **_unused):
    f32 = np.float32

    def fm(v, nch):
        return np.ascontiguousarray(np.asarray(v, f32).reshape(nch, P).T)

    base = {
        "w1T": np.ascontiguousarray(np.asarray(L0_ffn_w1, f32).T),
        "w2T": np.ascontiguousarray(np.asarray(L0_ffn_w2, f32).T),
        "Wq": np.ascontiguousarray(np.asarray(L0_attn_Wq, f32)),
        "Wkv": np.ascontiguousarray(np.asarray(L0_attn_Wkv, f32)),
        "Wo": np.ascontiguousarray(np.asarray(L0_attn_Wo, f32)),
        "pbw1": np.ascontiguousarray(np.asarray(L0_attn_pb_w1, f32)),
        "pbw2": np.ascontiguousarray(np.asarray(L0_attn_pb_w2, f32)),
        "lng": fm(L0_attn_ln_g, DCH), "lnb": fm(L0_attn_ln_b, DCH),
        "olng": fm(L0_attn_out_ln_g, DCH), "olnb": fm(L0_attn_out_ln_b, DCH),
        "b1f": fm(L0_ffn_b1, HCH), "b2f": fm(L0_ffn_b2, DCH),
        "pw0b0": np.ascontiguousarray(
            np.stack([np.asarray(L0_attn_pb_w0, f32).reshape(D),
                      np.asarray(L0_attn_pb_b0, f32)])),
        "pl0g": fm(L0_attn_pb_ln0_g, DCH), "pl0b": fm(L0_attn_pb_ln0_b, DCH),
        "pl1g": fm(L0_attn_pb_ln1_g, DCH), "pl1b": fm(L0_attn_pb_ln1_b, DCH),
        "pb1f": fm(L0_attn_pb_b1, DCH),
        "pb2r": np.ascontiguousarray(np.asarray(L0_attn_pb_b2, f32).reshape(HEADS, 1)),
        "posone": np.ascontiguousarray(
            np.stack([np.arange(S, dtype=f32), np.ones(S, f32)])),
        "nabr": np.ascontiguousarray(
            np.asarray(L0_attn_null_attn_bias, f32).reshape(1, HEADS)),
        "knullT": np.ascontiguousarray(np.asarray(L0_attn_null_kv, f32)[0].reshape(HD, 1)),
        "vnullr": np.ascontiguousarray(np.asarray(L0_attn_null_kv, f32)[1].reshape(1, HD)),
        "c1024": np.full((P, P), 1.0 / 1024.0, f32),
        "onesr": np.ones((1, HD), f32),
    }
    return base


def kernel(L0_attn_x, **kw):
    base = prep_base(**kw)
    xfull = np.asarray(L0_attn_x, np.float32)
    in_maps = []
    for b in range(8):
        m = dict(base)
        m["xT"] = np.ascontiguousarray(xfull[b].T)
        in_maps.append(m)

    nc = build_nc()
    res = run_bass_kernel_spmd(nc, in_maps, core_ids=list(range(8)))
    out = np.empty((8, S, D), np.float32)
    for b in range(8):
        out[b] = np.asarray(res.results[b]["o"], np.float32).T
    return out


# version 16 (final)
# speedup vs baseline: 1.0745x; 1.0745x over previous
import contextlib
import numpy as np
import concourse.bass as bass
import concourse.mybir as mybir
import bass_rust
from concourse.bass_utils import run_bass_kernel_spmd
from concourse.alu_op_type import AluOpType

F32 = mybir.dt.float32
BF16 = mybir.dt.bfloat16
FP16 = mybir.dt.float16
I32 = mybir.dt.int32
AF = mybir.ActivationFunctionType
OP = AluOpType

P = 128
S = 2048
D = 1024
HF = 4096
DCH = D // P
HCH = HF // P
NQ = 512
NQB = S // NQ
HALF = 1024
HEADS = 16
HD = 64
EPS = 1e-5

C1 = float(np.float32(1.0 / np.log(2.0)))
CHI = float(np.float32(np.log(2.0)))
CLO = float(np.float32(np.log(2.0) - np.float64(np.float32(np.log(2.0)))))

EBPAD = 640
EBW = EBPAD + S


def build_nc(dbg=()):
    nc = bass.Bass("TRN2", target_bir_lowering=False, detect_race_conditions=False)

    def di(name, shape):
        return nc.dram_tensor(name, shape, F32, kind="ExternalInput").ap()
    xT = di("xT", [D, S])
    w1T = di("w1T", [D, HF])
    w2T = di("w2T", [HF, D])
    Wq = di("Wq", [D, D])
    Wkv = di("Wkv", [D, 2 * HD])
    Wo = di("Wo", [D, D])
    pbw1 = di("pbw1", [D, D])
    pbw2 = di("pbw2", [D, HEADS])
    lng = di("lng", [P, DCH]); lnb = di("lnb", [P, DCH])
    olng = di("olng", [P, DCH]); olnb = di("olnb", [P, DCH])
    b1f = di("b1f", [P, HCH]); b2f = di("b2f", [P, DCH])
    pw0b0 = di("pw0b0", [2, D])
    pl0g = di("pl0g", [P, DCH]); pl0b = di("pl0b", [P, DCH])
    pl1g = di("pl1g", [P, DCH]); pl1b = di("pl1b", [P, DCH])
    pb1f = di("pb1f", [P, DCH])
    pb2r = di("pb2r", [HEADS, 1])
    posone = di("posone", [2, S])
    nabr = di("nabr", [1, HEADS])
    knullT = di("knullT", [HD, 1])
    vnullr = di("vnullr", [1, HD])
    c1024 = di("c1024", [P, P])
    onesr = di("onesr", [1, HD])

    o_out = nc.dram_tensor("o", [D, S], F32, kind="ExternalOutput").ap()
    ebtab = nc.dram_tensor("ebtab", [HEADS, EBW], BF16, kind="Internal").ap()
    wqTd = nc.dram_tensor("wqTd", [HF, D], BF16, kind="Internal").ap()
    w1Tb = nc.dram_tensor("w1Tb", [D, HF], BF16, kind="Internal").ap()
    Wqb = nc.dram_tensor("Wqb", [D, D], BF16, kind="Internal").ap()
    Wkvb = nc.dram_tensor("Wkvb", [D, 2 * HD], BF16, kind="Internal").ap()
    Wob = nc.dram_tensor("Wob", [D, D], BF16, kind="Internal").ap()
    pbw1b = nc.dram_tensor("pbw1b", [D, D], BF16, kind="Internal").ap()
    pbw2b = nc.dram_tensor("pbw2b", [D, HEADS], BF16, kind="Internal").ap()

    es = contextlib.ExitStack()
    plan = {k: [] for k in "TVSYG"}
    cnt = {}

    class Sem:
        def __init__(self, name):
            self.h = es.enter_context(nc.semaphore(name))
            cnt[name] = 0
            self.name = name
        @property
        def c(self):
            return cnt[self.name]
        def bump(self, k):
            cnt[self.name] += k
            return cnt[self.name]

    def wait(eng, sem, val):
        if val > 0:
            plan[eng].append(lambda e, s=sem, v=val: e.wait_ge(s.h, v))

    with es:
        def sbt(name, shape, dt):
            return es.enter_context(nc.sbuf_tensor(name, shape, dt))

        x = sbt("x", [P, DCH * S], BF16)
        xn = sbt("xn", [P, DCH * S], BF16)
        arena = sbt("arena", [P, 32768], BF16)
        kvT = sbt("kvT", [P, S], BF16)
        kkB = sbt("kkB", [P, S], BF16)
        vblk = sbt("vblk", [P, 17 * 65], BF16)
        knull2 = sbt("knull2", [P, P], BF16)
        xsq = sbt("xsq", [P, DCH * NQ], BF16)
        B0 = sbt("B0", [P, HALF], BF16)
        B3 = sbt("B3", [P, HALF], BF16)
        B1 = sbt("B1", [P, HALF], BF16)
        B2 = sbt("B2", [P, HALF], BF16)
        H0 = sbt("H0", [P, HALF], FP16)
        H1 = sbt("H1", [P, HALF], FP16)
        H2 = sbt("H2", [P, HALF], FP16)
        H3 = sbt("H3", [P, HALF], FP16)
        I0 = sbt("I0", [P, HALF], I32)
        F0 = sbt("F0", [P, HALF], F32)
        F1 = sbt("F1", [P, HALF], F32)
        F2 = sbt("F2", [P, HALF], F32)
        F3 = sbt("F3", [P, HALF], F32)
        mu_s = sbt("mu_s", [P, NQ], F32)
        t0_s = sbt("t0_s", [P, NQ], F32)
        t1_s = sbt("t1_s", [P, NQ], F32)
        rr_s = sbt("rr_s", [P, NQ], F32)
        pA = sbt("pA", [P, NQ], BF16)
        pB = sbt("pB", [P, NQ], BF16)
        peA = sbt("peA", [P, NQ], BF16)
        peB = sbt("peB", [P, NQ], BF16)
        peA2 = sbt("peA2", [P, NQ], BF16)
        peB2 = sbt("peB2", [P, NQ], BF16)
        ebtA = [sbt(f"ebtA{i}", [P, NQ], BF16) for i in range(2)]
        ebtB = [sbt(f"ebtB{i}", [P, NQ], BF16) for i in range(2)]
        tav = sbt("tav", [HD, NQ], BF16)
        tavn = sbt("tavn", [HD, NQ], BF16)
        c1024b = sbt("c1024b", [P, P], BF16)
        onesrF = sbt("onesrF", [1, HD], F32)
        lngS = sbt("lngS", [P, DCH], F32); lnbS = sbt("lnbS", [P, DCH], F32)
        olngS = sbt("olngS", [P, DCH], F32); olnbS = sbt("olnbS", [P, DCH], F32)
        b1S = sbt("b1S", [P, HCH], F32); b2S = sbt("b2S", [P, DCH], F32)
        bqS = sbt("bqS", [P, DCH], F32)
        pw0b0S = sbt("pw0b0S", [2, D], BF16)
        pl0gS = sbt("pl0gS", [P, DCH], F32); pl0bS = sbt("pl0bS", [P, DCH], F32)
        pl1gS = sbt("pl1gS", [P, DCH], F32); pl1bS = sbt("pl1bS", [P, DCH], F32)
        pb1S = sbt("pb1S", [P, DCH], F32)
        pb2S = sbt("pb2S", [HEADS, 1], F32)
        nabS = sbt("nabS", [1, HEADS], F32)
        nullm = sbt("nullm", [P, HEADS], F32)
        vnS = sbt("vnS", [1, HD], BF16)
        b0c = sbt("b0c", [P, 1], F32)
        epsc = sbt("epsc", [P, 1], F32)
        wb = [sbt(f"wb{i}", [P, NQ], BF16) for i in range(2)]

        psum = [es.enter_context(nc.psum_tensor(f"ps{i}", [P, NQ], F32))
                for i in range(8)]

        dS = Sem("dS")
        dG = Sem("dG")
        sT = Sem("sT")
        sV = Sem("sV")
        sS = Sem("sS")

        def dma(queue, out, in_, sem):
            tgt = sem.bump(16)
            plan[queue].append(
                lambda e, o=out, i=in_, s=sem: e.dma_start(out=o, in_=i).then_inc(s.h, 16))
            return tgt

        def dma_tr(queue, out, in_, sem):
            tgt = sem.bump(16)
            plan[queue].append(
                lambda e, o=out, i=in_, s=sem: e.dma_start_transpose(out=o, in_=i).then_inc(s.h, 16))
            return tgt

        def _mk(plankey, sem, meth, kw):
            tgt = sem.bump(1) if sem else None
            def fn(e, meth=meth, kw=kw, sem=sem):
                ins = getattr(e, meth)(**kw)
                if sem is not None:
                    ins.then_inc(sem.h, 1)
            plan[plankey].append(fn)
            return tgt

        def v_op(meth, sem=None, **kw):
            return _mk("V", sem, meth, kw)

        def s_op(meth, sem=None, **kw):
            return _mk("S", sem, meth, kw)

        def t_op(meth, sem=None, **kw):
            return _mk("T", sem, meth, kw)

        def toep(off):
            a = ebtab.copy()
            a.ap = bass_rust.VecI64Pair([(1, P), (1, NQ)])
            a.offset = off
            return a

        def dump_sb(name, src_tensor, cols, parts=P, dt=BF16):
            if name not in dbg:
                return
            t = nc.dram_tensor("dbg_" + name, [parts, cols], dt,
                               kind="ExternalOutput").ap()
            for e in "TVS":
                wait("Y", sT, sT.c); wait("Y", sV, sV.c); wait("Y", sS, sS.c)
            wait("Y", dG, dG.c)
            dma("Y", t, src_tensor[0:parts, 0:cols], dS)
            tgt = dS.c
            wait("V", dS, tgt); wait("S", dS, tgt); wait("T", dS, tgt)
            wait("G", dS, tgt)

        def dump_x(name):
            if name not in dbg:
                return
            t = nc.dram_tensor("dbg_" + name, [D, S], BF16,
                               kind="ExternalOutput").ap()
            wait("Y", sT, sT.c); wait("Y", sV, sV.c); wait("Y", sS, sS.c)
            for c in range(DCH):
                dma("Y", t[c * P:(c + 1) * P, :], x[:, c * S:(c + 1) * S], dS)
            tgt = dS.c
            wait("V", dS, tgt); wait("S", dS, tgt); wait("T", dS, tgt)
            wait("G", dS, tgt)

        for c in range(DCH):
            dma("G", x[:, c * S:(c + 1) * S], xT[c * P:(c + 1) * P, :], dG)
        dma("G", c1024b[:], c1024, dG)
        dma("G", pw0b0S[:], pw0b0, dG)
        dma("G", kvT[0:2, :], posone, dG)
        dma("G", vnS[:], vnullr, dG)
        dma("G", knull2[0:HD, 0:1], knullT, dG)
        dma("G", knull2[HD:P, 0:1], knullT, dG)
        ld_g = dG.c
        for dst, src in [(lngS, lng), (lnbS, lnb), (olngS, olng), (olnbS, olnb),
                         (b1S, b1f), (b2S, b2f), (pl0gS, pl0g), (pl0bS, pl0b),
                         (pl1gS, pl1g), (pl1bS, pl1b), (pb1S, pb1f), (pb2S, pb2r),
                         (nabS, nabr), (onesrF, onesr)]:
            dma("Y", dst[:], src, dS)
        ld_sync = dS.c

        def fn_preload(e):
            e.add_instruction(mybir.InstLoadActFuncSet(
                name=e.bass.get_next_instruction_name(),
                act_func_set_id=6, ins=[], outs=[]))
        plan["S"].append(fn_preload)

        wait("V", dG, ld_g)
        wait("V", dS, ld_sync)
        v_op("memset", ap=b0c[:], constant=0.0)
        v_op("memset", ap=epsc[:], constant=float(EPS))
        v_op("memset", ap=kvT[32:48, 0:EBPAD], constant=0.0)
        v_op("memset", ap=vblk[:], constant=1.0)
        v_op("memset", ap=vblk[:, 16 * 65:17 * 65], constant=0.0)
        v_op("memset", ap=vblk[0:1, 16 * 65 + HD:16 * 65 + HD + 1], constant=1.0)
        v_op("memset", ap=knull2[:, 1:P], constant=0.0)
        v_op("memset", ap=nullm[:], constant=0.0)
        init_v = v_op("tensor_copy", sem=sV,
                      out=vblk[0:1, 16 * 65:16 * 65 + HD], in_=vnS[:])
        wait("S", dS, ld_sync)
        wait("S", sV, init_v)
        s_init = s_op("activation", sem=sS, out=nabS[:], in_=nabS[:], func=AF.Exp,
                      scale=1.0, bias=b0c[0:1, :])
        wait("V", sS, s_init)
        init_v2 = v_op("tensor_copy", sem=sV, out=nullm[0:1, :], in_=nabS[:])
        dump_sb("nullmd", nullm, HEADS, dt=F32)
        dump_sb("nabSd", nabS, HEADS, parts=1, dt=F32)

        def chain_core(n, prescale_ln, sem_u):
            ltgt = s_op("activation", sem=sS, out=F0[:, :n], in_=B1[:, :n],
                        func=AF.Ln, scale=prescale_ln, bias=b0c[:])
            wait("V", sS, ltgt)
            v_op("tensor_scalar", out=F1[:, :n], in0=F0[:, :n], scalar1=-C1,
                 scalar2=1059.5, op0=OP.mult, op1=OP.add)
            v_op("tensor_copy", out=I0[:, :n], in_=F1[:, :n])
            v_op("tensor_copy", out=H1[:, :n], in_=I0[:, :n])
            v_op("tensor_tensor", out=H2[:, :n], in0=F1[:, :n], in1=H1[:, :n],
                 op=OP.subtract)
            v_op("scalar_tensor_tensor", out=H0[:, :n], in0=H2[:, :n],
                 scalar=0.5, in1=H1[:, :n], op0=OP.is_gt, op1=OP.add)
            v_op("scalar_tensor_tensor", out=H1[:, :n], in0=H2[:, :n],
                 scalar=-0.5, in1=H0[:, :n], op0=OP.is_ge, op1=OP.add)
            v_op("tensor_scalar", out=H2[:, :n], in0=H1[:, :n], scalar1=1.0,
                 scalar2=-1060.0, op0=OP.mult, op1=OP.add)
            v_op("tensor_scalar", out=F1[:, :n], in0=H2[:, :n], scalar1=CHI,
                 scalar2=None, op0=OP.mult)
            utgt = v_op("scalar_tensor_tensor", sem=sem_u, out=F2[:, :n],
                        in0=H2[:, :n], scalar=CLO, in1=F1[:, :n],
                        op0=OP.mult, op1=OP.add)
            wait("S", sem_u, utgt)
            s_op("activation", out=B1[:, :n], in_=F2[:, :n], func=AF.Exp,
                 scale=1.0, bias=b0c[:])
            etgt = s_op("activation", sem=sS, out=B2[:, :n], in_=F2[:, :n],
                        func=AF.Exp, scale=-1.0, bias=b0c[:])
            wait("V", sS, etgt)
            return etgt

        def chain_m_unsigned(r_src, n, q_out, sem):
            v_op("tensor_tensor", out=H3[:, :n], in0=r_src, in1=B1[:, :n], op=OP.mult)
            v_op("tensor_copy", out=I0[:, :n], in_=H3[:, :n])
            v_op("tensor_copy", out=H1[:, :n], in_=I0[:, :n])
            v_op("tensor_tensor", out=H2[:, :n], in0=H3[:, :n], in1=H1[:, :n],
                 op=OP.subtract)
            v_op("scalar_tensor_tensor", out=H0[:, :n], in0=H2[:, :n], scalar=0.5,
                 in1=H1[:, :n], op0=OP.is_gt, op1=OP.add)
            v_op("scalar_tensor_tensor", out=H1[:, :n], in0=H2[:, :n], scalar=-0.5,
                 in1=H0[:, :n], op0=OP.is_ge, op1=OP.add)
            return v_op("scalar_tensor_tensor", sem=sem, out=q_out, in0=H1[:, :n],
                        scalar=-1.0, in1=B2[:, :n], op0=OP.add, op1=OP.mult)

        def chain_m_signed(x_src, n, q_out, sem):
            v_op("tensor_tensor", out=F1[:, :n], in0=x_src, in1=B1[:, :n], op=OP.mult)
            v_op("tensor_copy", out=I0[:, :n], in_=F1[:, :n])
            v_op("tensor_copy", out=H1[:, :n], in_=I0[:, :n])
            v_op("tensor_tensor", out=F2[:, :n], in0=F1[:, :n], in1=H1[:, :n],
                 op=OP.subtract)
            v_op("scalar_tensor_tensor", out=H0[:, :n], in0=F2[:, :n], scalar=0.5,
                 in1=H1[:, :n], op0=OP.is_gt, op1=OP.add)
            v_op("scalar_tensor_tensor", out=H1[:, :n], in0=F2[:, :n], scalar=-0.5,
                 in1=H0[:, :n], op0=OP.is_ge, op1=OP.add)
            return v_op("scalar_tensor_tensor", sem=sem, out=q_out, in0=H1[:, :n],
                        scalar=-1.0, in1=B2[:, :n], op0=OP.add, op1=OP.mult)

        def getsl(src, c, qb):
            return src[:, c * S + qb * NQ: c * S + qb * NQ + NQ]

        def ln_stats(src, qb):
            wait("S", sT, sT.c)
            for c in range(DCH):
                s_op("activation", sem=sS if c == DCH - 1 else None,
                     out=xsq[:, c * NQ:(c + 1) * NQ], in_=getsl(src, c, qb),
                     func=AF.Square, scale=1.0, bias=b0c[:])
            wait("T", sS, sS.c)
            tgt = None
            for c in range(DCH):
                t_op("matmul", out=psum[6][:], lhsT=c1024b[:], rhs=getsl(src, c, qb),
                     start=(c == 0), stop=(c == DCH - 1))
                tgt = t_op("matmul", sem=sT if c == DCH - 1 else None,
                           out=psum[7][:], lhsT=c1024b[:],
                           rhs=xsq[:, c * NQ:(c + 1) * NQ],
                           start=(c == 0), stop=(c == DCH - 1))
            return tgt

        def ln_statmath(composed, t_tgt):
            wait("V", sT, t_tgt)
            v_op("tensor_copy", out=mu_s[:], in_=psum[6][:])
            v_op("tensor_copy", out=t1_s[:], in_=psum[7][:])
            v_op("tensor_tensor", out=t0_s[:], in0=mu_s[:], in1=mu_s[:], op=OP.mult)
            vtgt = v_op("tensor_tensor", sem=sV, out=t1_s[:], in0=t1_s[:],
                        in1=t0_s[:], op=OP.subtract)
            wait("S", sV, vtgt)
            s_op("activation", out=t0_s[:], in_=t1_s[:], func=AF.Ln, scale=1.0,
                 bias=epsc[:])
            stgt = s_op("activation", sem=sS, out=rr_s[:], in_=t0_s[:], func=AF.Exp,
                        scale=-0.5, bias=b0c[:])
            wait("V", sS, stgt)
            if composed:
                v_op("tensor_scalar", out=t0_s[:], in0=t1_s[:], scalar1=float(EPS),
                     scalar2=None, op0=OP.add)
                v_op("reciprocal", out=t0_s[:], in_=t0_s[:])
                vtgt2 = v_op("tensor_tensor", sem=sV, out=t1_s[:], in0=t1_s[:],
                             in1=t0_s[:], op=OP.mult)
                wait("S", sV, vtgt2)
                s_op("activation", out=t0_s[:], in_=t1_s[:], func=AF.Ln, scale=1.0,
                     bias=epsc[:])
                stgt2 = s_op("activation", sem=sS, out=t1_s[:], in_=t0_s[:],
                             func=AF.Exp, scale=-0.5, bias=b0c[:])
                wait("V", sS, stgt2)
                v_op("tensor_tensor", out=rr_s[:], in0=rr_s[:], in1=t1_s[:],
                     op=OP.mult)

        def ln_normalize(src, qb, dst, g=None, b=None):
            for c in range(DCH):
                sl = slice(c * S + qb * NQ, c * S + qb * NQ + NQ)
                v_op("scalar_tensor_tensor", out=F1[:, :NQ], in0=src[:, sl],
                     scalar=1.0, in1=mu_s[:], op0=OP.mult, op1=OP.subtract)
                last = (c == DCH - 1)
                if g is None:
                    v_op("tensor_tensor", sem=sV if last else None,
                         out=dst[:, sl], in0=F1[:, :NQ], in1=rr_s[:], op=OP.mult)
                else:
                    v_op("tensor_tensor", out=F1[:, :NQ], in0=F1[:, :NQ],
                         in1=rr_s[:], op=OP.mult)
                    v_op("tensor_scalar", sem=sV if last else None,
                         out=dst[:, sl], in0=F1[:, :NQ],
                         scalar1=g[:, c:c + 1], scalar2=b[:, c:c + 1],
                         op0=OP.mult, op1=OP.add)

        def precast(slabs):
            fbufs = [F3, F2]
            bbufs = [B0, B2]
            cp = [0, 0]
            st = [0, 0]
            for i, (src, dst, w) in enumerate(slabs):
                j = i % 2
                wait("Y", sS, cp[j])
                ld = dma("Y", fbufs[j][:, :w], src, dS)
                wait("S", dS, ld)
                wait("S", dS, st[j])
                cp[j] = s_op("activation", sem=sS, out=bbufs[j][:, :w],
                             in_=fbufs[j][:, :w], func=AF.Copy, scale=1.0,
                             bias=0.0)
                wait("Y", sS, cp[j])
                st[j] = dma("Y", dst, bbufs[j][:, :w], dS)
            return max(st)

        def slabs_of(src, dst, rows, cols):
            out = []
            for r0 in range(0, rows, P):
                for c0 in range(0, cols, HALF):
                    w = min(HALF, cols - c0)
                    out.append((src[r0:r0 + P, c0:c0 + w],
                                dst[r0:r0 + P, c0:c0 + w], w))
            return out

        wcast_tgt = [0]
        pb_cast = precast(slabs_of(pbw1, pbw1b, D, D)
                          + slabs_of(pbw2, pbw2b, D, HEADS))

        wait("T", dG, ld_g)
        wait("T", sV, init_v2)
        h0 = arena
        H1OFF = DCH * S

        for c in range(DCH):
            for qb in range(NQB):
                bank = (c * NQB + qb) % 4
                if c * NQB + qb >= 4:
                    wait("T", sS, sS.c)
                ttgt = t_op("matmul", sem=sT, out=psum[bank][:],
                            lhsT=pw0b0S[:, c * P:(c + 1) * P],
                            rhs=kvT[0:2, qb * NQ:(qb + 1) * NQ],
                            start=True, stop=True)
                wait("S", sT, ttgt)
                s_op("activation", sem=sS, out=getsl(h0, c, qb),
                     in_=psum[bank][:], func=AF.Copy, scale=1.0, bias=0.0)
        wait("T", sS, sS.c)
        wait("V", sS, sS.c)

        class OffT:
            def __init__(self, t, off):
                self.t, self.off = t, off
            def __getitem__(self, key):
                p, sl = key
                return self.t[p, slice(sl.start + self.off, sl.stop + self.off)]
        OffT2 = OffT

        def posmlp_ln_silu(src, gS, bS):
            for qb in range(NQB):
                ttgt = ln_stats(src, qb)
                ln_statmath(False, ttgt)
                for c in range(DCH):
                    sl = slice(c * S + qb * NQ, c * S + qb * NQ + NQ)
                    v_op("scalar_tensor_tensor", out=F1[:, :NQ], in0=src[:, sl],
                         scalar=1.0, in1=mu_s[:], op0=OP.mult, op1=OP.subtract)
                    v_op("tensor_tensor", out=F1[:, :NQ], in0=F1[:, :NQ],
                         in1=rr_s[:], op=OP.mult)
                    vtgt = v_op("tensor_scalar", sem=sV, out=F2[:, :NQ],
                                in0=F1[:, :NQ], scalar1=gS[:, c:c + 1],
                                scalar2=bS[:, c:c + 1], op0=OP.mult, op1=OP.add)
                    wait("S", sV, vtgt)
                    stgt = s_op("activation", sem=sS, out=F1[:, :NQ], in_=F2[:, :NQ],
                                func=AF.Exp, scale=-1.0, bias=b0c[:])
                    wait("V", sS, stgt)
                    v_op("tensor_scalar", out=F1[:, :NQ], in0=F1[:, :NQ],
                         scalar1=1.0, scalar2=None, op0=OP.add)
                    v_op("reciprocal", out=F1[:, :NQ], in_=F1[:, :NQ])
                    v_op("tensor_tensor", sem=sV, out=xn[:, sl], in0=F2[:, :NQ],
                         in1=F1[:, :NQ], op=OP.mult)
            wait("T", sV, sV.c)

        dump_sb("h0d", arena, DCH * S)
        posmlp_ln_silu(h0, pl0gS, pl0bS)
        dump_sb("pm0", xn, DCH * S)

        wtgts = []
        wait("Y", sT, sT.c)
        wait("Y", dS, pb_cast)
        for c2 in range(DCH):
            for fc in range(DCH):
                buf = wb[(c2 * DCH + fc) % 2]
                k = c2 * DCH + fc
                if k >= 2:
                    wait("Y", sT, wtgts[k - 2])
                dtgt = dma("Y", buf[:, 0:P],
                           pbw1b[fc * P:(fc + 1) * P, c2 * P:(c2 + 1) * P], dS)
                wait("T", dS, dtgt)
                for qb in range(NQB):
                    t = t_op("matmul",
                             sem=sT if qb == NQB - 1 else None,
                             out=psum[qb][:], lhsT=buf[:, 0:P],
                             rhs=getsl(xn, fc, qb),
                             start=(fc == 0), stop=(fc == DCH - 1))
                wtgts.append(sT.c)
            wait("S", sT, sT.c)
            for qb in range(NQB):
                s_op("activation", sem=sS if qb == NQB - 1 else None,
                     out=arena[:, H1OFF + c2 * S + qb * NQ: H1OFF + c2 * S + qb * NQ + NQ],
                     in_=psum[qb][:], func=AF.Identity, scale=1.0,
                     bias=pb1S[:, c2:c2 + 1])
            wait("T", sS, sS.c)

        dump_sb("h1r", arena, 2 * DCH * S)
        posmlp_ln_silu(OffT(arena, H1OFF), pl1gS, pl1bS)
        dump_sb("pm1", xn, DCH * S)

        wtgts = []
        wait("Y", sT, sT.c)
        for qb in range(NQB):
            for fc in range(DCH):
                buf = wb[(qb * DCH + fc) % 2]
                k = qb * DCH + fc
                if k >= 2:
                    wait("Y", sT, wtgts[k - 2])
                dtgt = dma("Y", buf[:, 0:HEADS], pbw2b[fc * P:(fc + 1) * P, :], dS)
                wait("T", dS, dtgt)
                ttgt = t_op("matmul", sem=sT,
                            out=psum[4][0:HEADS, :], lhsT=buf[:, 0:HEADS],
                            rhs=getsl(xn, fc, qb),
                            start=(fc == 0), stop=(fc == DCH - 1))
                wtgts.append(ttgt)
            wait("S", sT, sT.c)
            s_op("activation", sem=sS, out=kkB[0:HEADS, qb * NQ:(qb + 1) * NQ],
                 in_=psum[4][0:HEADS, :], func=AF.Exp, scale=1.0, bias=pb2S[:])
            wait("T", sS, sS.c)
        eb_done = sS.c
        wait("Y", sS, eb_done)
        wait("Y", sV, init_v2)
        dma("Y", ebtab[:, 0:EBPAD], kvT[32:48, 0:EBPAD], dS)
        ebw_tgt = dma("Y", ebtab[:, EBPAD:EBW], kkB[0:HEADS, :], dS)

        n8 = P
        wait("S", sV, init_v2)
        v_op("memset", ap=t1_s[:, :n8], constant=1.0)
        padtgt = v_op("tensor_copy", sem=sV, out=t1_s[:, :DCH], in_=b2S[:])
        wait("S", sV, padtgt)
        atgt = s_op("activation", sem=sS, out=B0[:, :n8], in_=t1_s[:, :n8],
                    func=AF.Abs, scale=1.0, bias=b0c[:])
        wait("V", sS, atgt)
        pretgt = v_op("tensor_scalar", sem=sV, out=B1[:, :n8], in0=B0[:, :n8],
                      scalar1=2.0 ** -14, scalar2=None, op0=OP.max)
        wait("S", sV, pretgt)
        chain_core(n8, float(2.0 ** -10), sV)
        chain_m_signed(t1_s[:, :n8], n8, t0_s[:, :n8], None)
        bq_tgt = v_op("tensor_copy", sem=sV, out=bqS[:], in_=t0_s[:, :DCH])

        dump_sb("bqv", bqS, DCH, dt=F32)
        for ch in range(HCH):
            dtgt = dma("Y", F3[:], w2T[ch * P:(ch + 1) * P, :], dS)
            wait("S", dS, dtgt)
            n = HALF
            atgt = s_op("activation", sem=sS, out=B0[:, :n], in_=F3[:],
                        func=AF.Abs, scale=1.0, bias=b0c[:])
            wait("V", sS, atgt)
            pretgt = v_op("tensor_scalar", sem=sV, out=B1[:, :n], in0=B0[:, :n],
                          scalar1=2.0 ** -6, scalar2=None, op0=OP.max)
            wait("S", sV, pretgt)
            chain_core(n, 0.125, sV)
            qtgt = chain_m_signed(F3[:], n, B0[:, :n], sV)
            wait("G", sV, qtgt)
            wtg = dma("G", wqTd[ch * P:(ch + 1) * P, :], B0[:, :n], dG)
            wait("Y", sV, qtgt)
            wait("S", dG, wtg)
        wq_done_g = dG.c
        if "wq" in dbg:
            t_wq = nc.dram_tensor("dbg_wq", [HF, D], BF16,
                                  kind="ExternalOutput").ap()
            wait("Y", dG, wq_done_g)
            dma("Y", t_wq, wqTd, dS)

        wait("Y", sV, sV.c)
        wait("S", sV, sV.c)
        wait("S", dG, dG.c)
        wcast_tgt[0] = precast(
            slabs_of(w1T, w1Tb, D, HF) + slabs_of(Wq, Wqb, D, D)
            + slabs_of(Wkv, Wkvb, D, 2 * HD) + slabs_of(Wo, Wob, D, D))

        def ffn_call(tag=""):
            wait("S", sV, sV.c)
            for qb in range(NQB):
                ttgt = ln_stats(x, qb)
                ln_statmath(False, ttgt)
                ln_normalize(x, qb, xn)
            wait("T", sV, sV.c)
            dump_sb("xn_" + tag, xn, DCH * S)
            for half in range(2):
                off = half * HALF
                relu_tgts = []
                q_tgts = []
                wtgts = []
                wait("Y", sT, sT.c)
                wait("Y", dS, wcast_tgt[0])
                for hb in range(HCH):
                    pp = (hb % 2) * 2
                    if hb >= 2:
                        wait("T", sS, relu_tgts[hb - 2])
                    for g in range(2):
                        kg = hb * 2 + g
                        buf = wb[kg % 2]
                        if kg >= 2:
                            wait("Y", sT, wtgts[kg - 2])
                        src = w1Tb.copy()
                        src.ap = bass_rust.VecI64Pair(
                            [(HF, P), (P * HF, 4), (1, P)])
                        src.offset = g * 4 * P * HF + hb * P
                        dtgt = dma("Y", buf[:, 0:4 * P], src, dS)
                        wait("T", dS, dtgt)
                        for fi in range(4):
                            fc = g * 4 + fi
                            for hh in range(2):
                                t_op("matmul",
                                     sem=sT if (hh == 1 and fi == 3) else None,
                                     out=psum[pp + hh][:],
                                     lhsT=buf[:, fi * P:(fi + 1) * P],
                                     rhs=xn[:, fc * S + off + hh * NQ: fc * S + off + hh * NQ + NQ],
                                     start=(fc == 0), stop=(fc == DCH - 1))
                        wtgts.append(sT.c)
                    wait("S", sT, sT.c)
                    if hb >= 2:
                        wait("S", sV, q_tgts[hb - 2])
                    rb = B0 if hb % 2 == 0 else B3
                    s_op("activation", out=rb[:, 0:NQ], in_=psum[pp][:],
                         func=AF.Relu, scale=1.0, bias=b1S[:, hb:hb + 1])
                    rtgt = s_op("activation", sem=sS, out=rb[:, NQ:HALF],
                                in_=psum[pp + 1][:], func=AF.Relu, scale=1.0,
                                bias=b1S[:, hb:hb + 1])
                    relu_tgts.append(rtgt)
                    wait("V", sS, rtgt)
                    pretgt = v_op("tensor_scalar", sem=sV, out=B1[:, :HALF],
                                  in0=rb[:, :HALF], scalar1=2.0 ** -6,
                                  scalar2=None, op0=OP.max)
                    wait("S", sV, pretgt)
                    chain_core(HALF, 0.125, sV)
                    q_tgts.append(chain_m_unsigned(
                        rb[:, :HALF], HALF,
                        arena[:, hb * HALF:(hb + 1) * HALF], sV))
                wait("T", sV, sV.c)
                dump_sb("hq_%s_%d" % (tag, half), arena, HCH * HALF)
                out_tgts = []
                res_tgts = []
                wtgts = []
                wait("Y", sT, sT.c)
                wait("Y", dG, wq_done_g)
                for db in range(DCH):
                    pp = 4 + (db % 2) * 2
                    if db >= 2:
                        wait("T", sS, out_tgts[db - 2])
                    for g in range(8):
                        kg = db * 8 + g
                        buf = wb[kg % 2]
                        if kg >= 2:
                            wait("Y", sT, wtgts[kg - 2])
                        src = wqTd.copy()
                        src.ap = bass_rust.VecI64Pair(
                            [(D, P), (P * D, 4), (1, P)])
                        src.offset = g * 4 * P * D + db * P
                        dtgt = dma("Y", buf[:, 0:4 * P], src, dS)
                        wait("T", dS, dtgt)
                        for hi in range(4):
                            hb = g * 4 + hi
                            for hh in range(2):
                                t_op("matmul",
                                     sem=sT if (hh == 1 and hi == 3) else None,
                                     out=psum[pp + hh][:],
                                     lhsT=buf[:, hi * P:(hi + 1) * P],
                                     rhs=arena[:, hb * HALF + hh * NQ: hb * HALF + hh * NQ + NQ],
                                     start=(hb == 0), stop=(hb == HCH - 1))
                        wtgts.append(sT.c)
                    wait("S", sT, sT.c)
                    if db >= 1:
                        wait("S", sV, res_tgts[db - 1])
                    s_op("activation", out=F3[:, 0:NQ], in_=psum[pp][:],
                         func=AF.Identity, scale=1.0, bias=bqS[:, db:db + 1])
                    s_op("activation", out=F3[:, NQ:HALF], in_=psum[pp + 1][:],
                         func=AF.Identity, scale=1.0, bias=bqS[:, db:db + 1])
                    atgt = s_op("activation", sem=sS, out=B0[:, :HALF],
                                in_=F3[:, :HALF], func=AF.Abs, scale=1.0, bias=b0c[:])
                    out_tgts.append(atgt)
                    shad = ("shad" in dbg and tag == "f1" and half == 1
                            and db == 1)
                    if shad:
                        s_op("activation", out=peA[:], in_=psum[pp + 1][:],
                             func=AF.Copy, scale=1.0, bias=0.0)
                    if db == 0 and half == 0:
                        dump_sb("f3_" + tag, F3, HALF, dt=F32)
                    wait("V", sS, atgt)
                    pretgt = v_op("tensor_scalar", sem=sV, out=B1[:, :HALF],
                                  in0=B0[:, :HALF], scalar1=2.0 ** -14,
                                  scalar2=None, op0=OP.max)
                    wait("S", sV, pretgt)
                    chain_core(HALF, float(2.0 ** -10), sV)
                    chain_m_signed(F3[:, :HALF], HALF, F0[:, :HALF], None)
                    if shad:
                        v_op("tensor_copy", out=peB[:], in_=F3[:, NQ:HALF])
                        v_op("tensor_copy", out=pB[:], in_=F0[:, NQ:HALF])
                    for hh in range(2):
                        sl = slice(db * S + off + hh * NQ, db * S + off + hh * NQ + NQ)
                        rt = v_op("scalar_tensor_tensor",
                                  sem=sV if hh == 1 else None,
                                  out=x[:, sl], in0=F0[:, hh * NQ:(hh + 1) * NQ],
                                  scalar=0.5, in1=x[:, sl], op0=OP.mult, op1=OP.add)
                    res_tgts.append(rt)
                wait("T", sV, sV.c)
            if "shad" in dbg and tag == "f1":
                dump_sb("shA", peA, NQ)
                dump_sb("shB", peB, NQ)
                dump_sb("shQ", pB, NQ)

        def attention():
            wait("S", sV, sV.c)
            for qb in range(NQB):
                ttgt = ln_stats(x, qb)
                ln_statmath(True, ttgt)
                ln_normalize(x, qb, xn, g=lngS, b=lnbS)
            wait("T", sV, sV.c)
            dump_sb("xn_at", xn, DCH * S)
            wtgts = []
            wait("Y", sT, sT.c)
            for qc in range(DCH):
                for g in range(2):
                    kg = qc * 2 + g
                    buf = wb[kg % 2]
                    if kg >= 2:
                        wait("Y", sT, wtgts[kg - 2])
                    src = Wqb.copy()
                    src.ap = bass_rust.VecI64Pair([(D, P), (P * D, 4), (1, P)])
                    src.offset = g * 4 * P * D + qc * P
                    dtgt = dma("Y", buf[:, 0:4 * P], src, dS)
                    wait("T", dS, dtgt)
                    for fi in range(4):
                        fc = g * 4 + fi
                        for qb in range(NQB):
                            t_op("matmul",
                                 sem=sT if (qb == NQB - 1 and fi == 3) else None,
                                 out=psum[qb][:],
                                 lhsT=buf[:, fi * P:(fi + 1) * P],
                                 rhs=getsl(xn, fc, qb),
                                 start=(fc == 0), stop=(fc == DCH - 1))
                    wtgts.append(sT.c)
                wait("S", sT, sT.c)
                for qb in range(NQB):
                    s_op("activation", sem=sS if qb == NQB - 1 else None,
                         out=arena[:, qc * S + qb * NQ: qc * S + qb * NQ + NQ],
                         in_=psum[qb][:], func=AF.Copy, scale=0.125, bias=0.0)
                wait("T", sS, sS.c)
            wtgts = []
            wait("Y", sT, sT.c)
            for g in range(2):
                buf = wb[g % 2]
                src = Wkvb.copy()
                src.ap = bass_rust.VecI64Pair(
                    [(2 * HD, P), (P * 2 * HD, 4), (1, 2 * HD)])
                src.offset = g * 4 * P * 2 * HD
                dtgt = dma("Y", buf[:, 0:4 * 2 * HD], src, dS)
                wait("T", dS, dtgt)
                for fi in range(4):
                    fc = g * 4 + fi
                    for qb in range(NQB):
                        t_op("matmul",
                             sem=sT if (qb == NQB - 1 and fi == 3) else None,
                             out=psum[4 + qb][:],
                             lhsT=buf[:, fi * 2 * HD:(fi + 1) * 2 * HD],
                             rhs=getsl(xn, fc, qb),
                             start=(fc == 0), stop=(fc == DCH - 1))
                wtgts.append(sT.c)
            wait("V", sT, sT.c)
            wait("V", dS, ebw_tgt)
            for qb in range(NQB):
                for blk in range(NQ // 128):
                    col0 = qb * NQ + blk * 128
                    rev = kvT[:, col0:col0 + 128].copy()
                    pitch = kvT.ap().ap[0][0]
                    rev.ap = bass_rust.VecI64Pair([(pitch, P), (-1, 128)])
                    rev.offset = col0 + 127
                    v_op("tensor_copy",
                         sem=sV if (qb == NQB - 1 and blk == 3) else None,
                         out=rev, in_=psum[4 + qb][:, blk * 128:(blk + 1) * 128])
            kv_done = sV.c
            wait("Y", sV, kv_done)
            dma("Y", kkB[HD:P, :], kvT[0:HD, :], dS)
            kv_aux = dS.c
            wait("Y", sS, sS.c)
            wait("Y", sV, sV.c)
            for fc in range(DCH):
                wkv_tgt = dma("Y", B1[:, fc * HD:(fc + 1) * HD],
                              Wkvb[fc * P:(fc + 1) * P, HD:2 * HD], dS)
            wait("T", dS, wkv_tgt)
            rv_tgts = []
            mm_tgts = []
            ev_tgts = []
            for kb in range(16):
                dst = B0 if kb % 2 == 0 else B2
                dpitch = dst.ap().ap[0][0]
                if kb >= 2:
                    wait("V", sT, mm_tgts[kb - 2])
                for c in range(DCH):
                    rev = dst[:, c * P:(c + 1) * P].copy()
                    rev.ap = bass_rust.VecI64Pair([(dpitch, P), (-1, P)])
                    rev.offset = c * P + 127
                    rv = v_op("tensor_copy", sem=sV if c == DCH - 1 else None,
                              out=rev,
                              in_=xn[:, c * S + kb * P: c * S + (kb + 1) * P])
                rv_tgts.append(rv)
                wait("T", sV, rv_tgts[kb])
                if kb >= 2:
                    wait("T", sS, ev_tgts[kb - 2])
                for c in range(DCH):
                    mm = t_op("matmul", sem=sT if c == DCH - 1 else None,
                              out=psum[kb % 2][:, 0:HD],
                              lhsT=dst[:, c * P:(c + 1) * P],
                              rhs=B1[:, c * HD:(c + 1) * HD],
                              start=(c == 0), stop=(c == DCH - 1))
                mm_tgts.append(mm)
                wait("S", sT, mm_tgts[kb])
                ev_tgts.append(s_op("activation", sem=sS,
                                    out=vblk[:, kb * 65:kb * 65 + HD],
                                    in_=psum[kb % 2][:, 0:HD],
                                    func=AF.Copy, scale=1.0, bias=0.0))
            wait("T", sS, ev_tgts[-1])
            wait("T", dS, kv_aux)
            dump_sb("q", arena, DCH * S)
            dump_sb("kvT", kvT, S)
            dump_sb("vblk", vblk, 17 * 65)
            wait("Y", dS, kv_aux)
            dump_sb("kkBd", kkB, S)
            dump_sb("knull2d", knull2, P)
            WOC = 20480
            wait("Y", sT, sT.c)
            for c in range(DCH):
                dma("Y", arena[:, WOC + c * D: WOC + (c + 1) * D],
                    Wob[c * P:(c + 1) * P, :], dS)
            wo_tgt = dS.c
            wait("T", dS, wo_tgt)
            AVQ = 16384

            eb_targets = {}
            fix_prev = None
            pe_prev = [None]
            for qb in range(NQB):
                i0 = qb * NQ
                kbs = list(range(4 * qb + 4)) + [16]
                for c in range(DCH):
                    hA, hB = 2 * c, 2 * c + 1
                    exp_tgts = []
                    pe_tgts = []
                    av_tgts = []
                    eb_dmas = []
                    nki = len(kbs)

                    def emit_sim(ki, c=c, hA=hA, hB=hB):
                        kb = kbs[ki]
                        ebuf = ki % 2
                        if kb < 16:
                            if ki >= 2:
                                wait("Y", sV, pe_tgts[ki - 2])
                                wait("G", sV, pe_tgts[ki - 2])
                            elif pe_prev[0] is not None:
                                wait("Y", sV, pe_prev[0])
                                wait("G", sV, pe_prev[0])
                            j0 = kb * 128
                            eA = dma("Y", ebtA[ebuf][:],
                                     toep(hA * EBW + EBPAD + i0 - j0 - 127), dS)
                            eB = dma("G", ebtB[ebuf][:],
                                     toep(hB * EBW + EBPAD + i0 - j0 - 127), dG)
                            lA = kvT[0:HD, j0:j0 + 128]
                            lB = kkB[HD:P, j0:j0 + 128]
                        else:
                            eA = eB = None
                            lA = knull2[0:HD, :]
                            lB = knull2[HD:P, :]
                        eb_dmas.append((eA, eB))
                        if ki >= 1:
                            wait("T", sS, exp_tgts[ki - 1])
                        t_op("matmul", out=psum[0][:], lhsT=lA,
                             rhs=arena[0:HD, c * S + i0: c * S + i0 + NQ],
                             start=True, stop=True)
                        return t_op("matmul", sem=sT, out=psum[1][:], lhsT=lB,
                                    rhs=arena[HD:P, c * S + i0: c * S + i0 + NQ],
                                    start=True, stop=True)

                    sim_tgts = [emit_sim(0)]
                    for ki, kb in enumerate(kbs):
                        last = (ki == nki - 1)
                        pbA, pbB = pA, pB
                        peO = (peA, peB) if ki % 2 == 0 else (peA2, peB2)
                        wait("S", sT, sim_tgts[ki])
                        if ki >= 1:
                            wait("S", sV, pe_tgts[ki - 1])
                        elif pe_prev[0] is not None:
                            wait("S", sV, pe_prev[0])
                        s_op("activation", out=pbA[:], in_=psum[0][:],
                             func=AF.Exp, scale=1.0, bias=b0c[:])
                        etg = s_op("activation", sem=sS, out=pbB[:],
                                   in_=psum[1][:], func=AF.Exp, scale=1.0,
                                   bias=b0c[:])
                        exp_tgts.append(etg)
                        if ki + 1 < nki:
                            sim_tgts.append(emit_sim(ki + 1))
                        wait("V", sS, etg)
                        if ki >= 2:
                            wait("V", sT, av_tgts[ki - 2])
                        if kb < 16:
                            eA, eB = eb_dmas[ki]
                            wait("V", dS, eA)
                            wait("V", dG, eB)
                            v_op("tensor_tensor", out=peO[0][:], in0=pbA[:],
                                 in1=ebtA[ki % 2][:], op=OP.mult)
                            vtg = v_op("tensor_tensor", sem=sV, out=peO[1][:],
                                       in0=pbB[:], in1=ebtB[ki % 2][:],
                                       op=OP.mult)
                        else:
                            v_op("tensor_scalar", out=peO[0][:], in0=pbA[:],
                                 scalar1=nullm[:, hA:hA + 1], scalar2=None,
                                 op0=OP.mult)
                            vtg = v_op("tensor_scalar", sem=sV, out=peO[1][:],
                                       in0=pbB[:], scalar1=nullm[:, hB:hB + 1],
                                       scalar2=None, op0=OP.mult)
                        pe_tgts.append(vtg)
                        pe_prev[0] = vtg
                        wait("T", sV, vtg)
                        t_op("matmul", out=psum[2][0:65, :],
                             lhsT=vblk[:, kb * 65:(kb + 1) * 65], rhs=peO[0][:],
                             start=(ki == 0), stop=last)
                        av_tgts.append(t_op(
                            "matmul", sem=sT,
                            out=psum[3][0:65, :],
                            lhsT=vblk[:, kb * 65:(kb + 1) * 65], rhs=peO[1][:],
                            start=(ki == 0), stop=last))
                    av_tgt = av_tgts[-1]
                    wait("V", sT, av_tgt)
                    wait("S", sT, av_tgt)
                    vr = v_op("reciprocal", sem=sV, out=t0_s[0:1, :],
                              in_=psum[2][HD:HD + 1, :])
                    wait("T", sV, vr)
                    rA = t_op("matmul", sem=sT, out=psum[4][0:HD, :], lhsT=onesrF[:],
                              rhs=t0_s[0:1, :], start=True, stop=True)
                    sA = s_op("activation", sem=sS, out=tav[:], in_=psum[2][0:HD, :],
                              func=AF.Copy, scale=1.0, bias=0.0)
                    wait("V", sT, rA)
                    wait("V", sS, sA)
                    avnA = v_op("tensor_tensor", sem=sV,
                                out=arena[0:HD, AVQ + c * NQ: AVQ + (c + 1) * NQ],
                                in0=tav[:], in1=psum[4][0:HD, :], op=OP.mult)
                    vr = v_op("reciprocal", sem=sV, out=t0_s[0:1, :],
                              in_=psum[3][HD:HD + 1, :])
                    wait("T", sV, vr)
                    rB = t_op("matmul", sem=sT, out=psum[5][0:HD, :], lhsT=onesrF[:],
                              rhs=t0_s[0:1, :], start=True, stop=True)
                    wait("S", sV, avnA)
                    sB = s_op("activation", sem=sS, out=tav[:], in_=psum[3][0:HD, :],
                              func=AF.Copy, scale=1.0, bias=0.0)
                    wait("V", sT, rB)
                    wait("V", sS, sB)
                    if fix_prev is not None:
                        wait("V", dG, fix_prev)
                    vB = v_op("tensor_tensor", sem=sV, out=tavn[:], in0=tav[:],
                              in1=psum[5][0:HD, :], op=OP.mult)
                    wait("G", sV, vB)
                    fix = dma("G", arena[HD:P, AVQ + c * NQ: AVQ + (c + 1) * NQ],
                              tavn[:], dG)
                    fix_prev = fix
                    wait("T", sS, sB)
                    wait("T", sV, vB)
                dump_sb("av_q%d" % qb, OffT2(arena, AVQ), DCH * NQ)
                wait("T", dG, dG.c)
                o_tgts = []
                for db in range(DCH):
                    bank = 6 + db % 2
                    if db >= 2:
                        wait("T", sS, o_tgts[db - 2])
                    for cc in range(DCH):
                        t_op("matmul", sem=sT if cc == DCH - 1 else None,
                             out=psum[bank][:],
                             lhsT=arena[:, WOC + cc * D + db * P: WOC + cc * D + (db + 1) * P],
                             rhs=arena[:, AVQ + cc * NQ: AVQ + (cc + 1) * NQ],
                             start=(cc == 0), stop=(cc == DCH - 1))
                    wait("S", sT, sT.c)
                    so = s_op("activation", sem=sS,
                              out=xn[:, db * S + i0: db * S + i0 + NQ],
                              in_=psum[bank][:], func=AF.Copy, scale=1.0, bias=0.0)
                    o_tgts.append(so)
                wait("V", sT, sT.c)
                wait("G", sT, sT.c)
            o_done = sS.c
            wait("V", sS, o_done)
            dump_sb("ao", xn, DCH * S)
            for qb in range(NQB):
                ttgt = ln_stats(xn, qb)
                ln_statmath(False, ttgt)
                for c in range(DCH):
                    sl = slice(c * S + qb * NQ, c * S + qb * NQ + NQ)
                    v_op("scalar_tensor_tensor", out=F1[:, :NQ], in0=xn[:, sl],
                         scalar=1.0, in1=mu_s[:], op0=OP.mult, op1=OP.subtract)
                    v_op("tensor_tensor", out=F1[:, :NQ], in0=F1[:, :NQ],
                         in1=rr_s[:], op=OP.mult)
                    v_op("tensor_scalar", out=F2[:, :NQ], in0=F1[:, :NQ],
                         scalar1=olngS[:, c:c + 1], scalar2=olnbS[:, c:c + 1],
                         op0=OP.mult, op1=OP.add)
                    v_op("tensor_tensor", sem=sV if c == DCH - 1 else None,
                         out=x[:, sl], in0=x[:, sl], in1=F2[:, :NQ], op=OP.add)

        wait("T", dG, wq_done_g)
        wait("T", sV, bq_tgt)
        wait("S", sV, bq_tgt)
        if "eb" in dbg:
            t_eb = nc.dram_tensor("dbg_eb", [HEADS, EBW], BF16,
                                  kind="ExternalOutput").ap()
            wait("Y", dS, ebw_tgt)
            dma("Y", t_eb, ebtab, dS)
        ffn_call("f1")
        dump_x("x1")
        wait("Y", sV, sV.c)
        wait("Y", dS, ebw_tgt)
        attention()
        dump_x("x2")
        ffn_call("f3")

        obufs = [F0, F1]
        otgts = [0, 0]
        for k in range(2 * DCH):
            c, h = k // 2, k % 2
            buf = obufs[k % 2]
            wait("V", dS, otgts[k % 2])
            vtgt = v_op("tensor_copy", sem=sV, out=buf[:, :HALF],
                        in_=x[:, c * S + h * HALF: c * S + (h + 1) * HALF])
            wait("Y", sV, vtgt)
            otgts[k % 2] = dma("Y", o_out[c * P:(c + 1) * P, h * HALF:(h + 1) * HALF],
                               buf[:, :HALF], dS)
        wait("Y", dS, dS.c)

        block = es.enter_context(nc.Block())

        @block.sync
        def _(sync):
            for fn in plan["Y"]:
                fn(sync)

        @block.gpsimd
        def _(g):
            for fn in plan["G"]:
                fn(g)

        @block.tensor
        def _(t):
            for fn in plan["T"]:
                fn(t)

        @block.vector
        def _(v):
            for fn in plan["V"]:
                fn(v)

        @block.scalar
        def _(s):
            for fn in plan["S"]:
                fn(s)

    return nc


def prep_base(L0_attn_ln_g, L0_attn_ln_b, L0_attn_Wq, L0_attn_Wkv,
              L0_attn_null_kv, L0_attn_null_attn_bias, L0_attn_pb_w0,
              L0_attn_pb_b0, L0_attn_pb_ln0_g, L0_attn_pb_ln0_b, L0_attn_pb_w1,
              L0_attn_pb_b1, L0_attn_pb_ln1_g, L0_attn_pb_ln1_b, L0_attn_pb_w2,
              L0_attn_pb_b2, L0_attn_Wo, L0_attn_out_ln_g, L0_attn_out_ln_b,
              L0_ffn_w1, L0_ffn_b1, L0_ffn_w2, L0_ffn_b2, **_unused):
    f32 = np.float32

    def fm(v, nch):
        return np.ascontiguousarray(np.asarray(v, f32).reshape(nch, P).T)

    base = {
        "w1T": np.ascontiguousarray(np.asarray(L0_ffn_w1, f32).T),
        "w2T": np.ascontiguousarray(np.asarray(L0_ffn_w2, f32).T),
        "Wq": np.ascontiguousarray(np.asarray(L0_attn_Wq, f32)),
        "Wkv": np.ascontiguousarray(np.asarray(L0_attn_Wkv, f32)),
        "Wo": np.ascontiguousarray(np.asarray(L0_attn_Wo, f32)),
        "pbw1": np.ascontiguousarray(np.asarray(L0_attn_pb_w1, f32)),
        "pbw2": np.ascontiguousarray(np.asarray(L0_attn_pb_w2, f32)),
        "lng": fm(L0_attn_ln_g, DCH), "lnb": fm(L0_attn_ln_b, DCH),
        "olng": fm(L0_attn_out_ln_g, DCH), "olnb": fm(L0_attn_out_ln_b, DCH),
        "b1f": fm(L0_ffn_b1, HCH), "b2f": fm(L0_ffn_b2, DCH),
        "pw0b0": np.ascontiguousarray(
            np.stack([np.asarray(L0_attn_pb_w0, f32).reshape(D),
                      np.asarray(L0_attn_pb_b0, f32)])),
        "pl0g": fm(L0_attn_pb_ln0_g, DCH), "pl0b": fm(L0_attn_pb_ln0_b, DCH),
        "pl1g": fm(L0_attn_pb_ln1_g, DCH), "pl1b": fm(L0_attn_pb_ln1_b, DCH),
        "pb1f": fm(L0_attn_pb_b1, DCH),
        "pb2r": np.ascontiguousarray(np.asarray(L0_attn_pb_b2, f32).reshape(HEADS, 1)),
        "posone": np.ascontiguousarray(
            np.stack([np.arange(S, dtype=f32), np.ones(S, f32)])),
        "nabr": np.ascontiguousarray(
            np.asarray(L0_attn_null_attn_bias, f32).reshape(1, HEADS)),
        "knullT": np.ascontiguousarray(np.asarray(L0_attn_null_kv, f32)[0].reshape(HD, 1)),
        "vnullr": np.ascontiguousarray(np.asarray(L0_attn_null_kv, f32)[1].reshape(1, HD)),
        "c1024": np.full((P, P), 1.0 / 1024.0, f32),
        "onesr": np.ones((1, HD), f32),
    }
    return base


def kernel(L0_attn_x, **kw):
    base = prep_base(**kw)
    xfull = np.asarray(L0_attn_x, np.float32)
    in_maps = []
    for b in range(8):
        m = dict(base)
        m["xT"] = np.ascontiguousarray(xfull[b].T)
        in_maps.append(m)

    nc = build_nc()
    res = run_bass_kernel_spmd(nc, in_maps, core_ids=list(range(8)))
    out = np.empty((8, S, D), np.float32)
    for b in range(8):
        out[b] = np.asarray(res.results[b]["o"], np.float32).T
    return out
